# version 27
# baseline (speedup 1.0000x reference)
import math
import numpy as np
import ml_dtypes
import concourse.bass as bass
import concourse.mybir as mybir

F32 = mybir.dt.float32
BF16 = mybir.dt.bfloat16
I32 = mybir.dt.int32
AF = mybir.ActivationFunctionType
OP = mybir.AluOpType
AX = mybir.AxisListType
BF = ml_dtypes.bfloat16

C_THR = np.float32(0.501953125)


def derive(cfg):
    c = dict(cfg)
    c['DM'] = c['NH'] * 128
    c['NK'] = c['DM'] // 128
    c['SPS'] = c['NCORES'] // c['B']
    c['SPAN'] = c['S'] // c['SPS']
    c['NCH'] = c['SPAN'] // 128
    c['PSPAN'] = c['SPAN'] + 3
    c['FHC'] = c['FH'] // 128
    c['DSH'] = c['DM'] // c['NCORES']
    c['FSH'] = c['FH'] // c['NCORES']
    c['NT'] = c['SPAN'] // 128
    ts = []
    col = 0
    while col < c['PSPAN']:
        w = min(512, c['PSPAN'] - col)
        ts.append((col, w)); col += w
    c['TSUBS'] = ts
    ts2 = []
    col = 0
    while col < c['SPAN']:
        w = min(512, c['SPAN'] - col)
        ts2.append((col, w)); col += w
    c['TOKSUBS'] = ts2
    c['HSUP'] = min(4, c['FHC'])
    c['NSUP'] = c['FHC'] // c['HSUP']
    assert c['SPAN'] % 128 == 0 and c['FH'] % 128 == 0 and c['DM'] % 128 == 0
    assert c['DSH'] % 128 == 0 and c['FSH'] % 128 == 0
    return c


class Prog:
    ENGS = ("sync", "scalar", "vector", "tensor", "gpsimd")

    def __init__(self):
        self.ops = {e: [] for e in self.ENGS}
        self.val = {}
        self.semnames = []
        for s in ("d_sync", "d_scalar", "d_gpsimd", "t_t", "t_v", "t_s", "cc"):
            self.sem(s)

    def sem(self, name):
        if name not in self.val:
            self.semnames.append(name)
            self.val[name] = 0

    def v(self, s):
        return self.val[s]

    def emit(self, eng, fn=None, wait=(), inc=(), tag=''):
        for s, n in inc:
            self.val[s] += n
        self.ops[eng].append((fn, tuple(wait), tuple(inc), tag))

    def vec(self, fn, wait=(), tag=''):
        w = [('t_v', self.v('t_v'))] + [x for x in wait if x]
        self.emit('vector', fn, wait=w, inc=[('t_v', 1)], tag=tag)
        return ('t_v', self.v('t_v'))

    def sca(self, fn, wait=(), tag=''):
        w = [('t_s', self.v('t_s'))] + [x for x in wait if x]
        self.emit('scalar', fn, wait=w, inc=[('t_s', 1)], tag=tag)
        return ('t_s', self.v('t_s'))

    def ten(self, fn, wait=(), tag=''):
        self.emit('tensor', fn, wait=[x for x in wait if x], inc=[('t_t', 1)], tag=tag)
        return ('t_t', self.v('t_t'))

    def guards(self):
        return [(s, self.v(s)) for s in self.semnames if self.v(s) > 0]

    def barrier(self):
        g = self.guards()
        self.vec(lambda e, S: e.engine_nop(), wait=g)
        self.sca(lambda e, S: e.nop(), wait=g)

    def dma(self, q, out_ap, in_ap, wait=(), sem=None, tag=''):
        s = sem or ('d_' + q)
        self.sem(s)
        self.emit(q, lambda e, S: e.dma_start(out=out_ap, in_=in_ap),
                  wait=[x for x in wait if x], inc=[(s, 16)], tag=tag)
        return (s, self.v(s))

    def replay(self, block, sems, nc_scope=None):
        for name in self.ENGS:
            ops = self.ops[name]

            def body(eng, _ops=ops, _nc=getattr(block, 'nc', None)):
                for fn, waits, incs, tag in _ops:
                    for s, v_ in waits:
                        eng.wait_ge(sems[s], v_)
                    if tag and nc_scope is not None:
                        with nc_scope(tag):
                            r = fn(eng, sems) if fn is not None else None
                    else:
                        r = fn(eng, sems) if fn is not None else None
                    if incs:
                        for s, n in incs:
                            r = r.then_inc(sems[s], n)
            getattr(block, name)(body)


class Arena:

    def __init__(self, t, total_f32, name):
        self.t, self.total, self.name = t, total_f32, name
        self.off = 0
        self.marks = []
        self.peak = 0

    def alloc(self, n, tag=""):
        n = int(n)
        a = (self.off + 7) & ~7
        assert a + n <= self.total, f"{self.name} overflow: {tag} {a}+{n}>{self.total}"
        self.off = a + n
        self.peak = max(self.peak, self.off)
        return self.t[:, a:a + n]

    def push(self):
        self.marks.append(self.off)

    def pop(self):
        self.off = self.marks.pop()


class ArenaB:

    def __init__(self, base):
        self.base = base

    def alloc(self, n, tag=""):
        f = self.base.alloc((int(n) + 1) // 2, tag + "_b")
        import concourse.mybir as _mb
        return f.bitcast(_mb.dt.bfloat16)[:, 0:int(n)]

    def push(self):
        self.base.push()

    def pop(self):
        self.base.pop()


class Bank:
    def __init__(self, ap):
        self.ap = ap
        self.w = None
        self.r = None

    def wwait(self):
        return [x for x in (self.r, self.w) if x]

    def rwait(self):
        return [self.w] if self.w else []


AF32_N = 35800


def build_nc(cfg, dbg_outputs=(), phases=('p0', 'p1', 'p2', 'ag2', 'p34')):
    c = derive(cfg)
    nc = bass.Bass("TRN2", target_bir_lowering=False, debug=False)
    P = Prog()
    dram = {}

    def din(name, shape, dt=F32):
        dram[name] = nc.dram_tensor(name, list(shape), dt, kind="ExternalInput").ap()

    def dint(name, shape, dt=F32):
        kind = "ExternalOutput" if name in dbg_outputs else "Internal"
        dram[name] = nc.dram_tensor(name, list(shape), dt, kind=kind).ap()

    NH, DM, NK, SPAN, PSPAN, NCH = c['NH'], c['DM'], c['NK'], c['SPAN'], c['PSPAN'], c['NCH']
    FH, NCORES = c['FH'], c['NCORES']

    din("x_span", [SPAN, DM]); din("x_halo", [3, DM])
    din("wq", [DM, DM]); din("wk", [DM, DM]); din("wv", [DM, DM]); din("wo", [DM, DM])
    din("wb", [NH, DM])
    din("convw", [128, 12 * NK])
    din("rmsw", [128, 1])
    din("ident", [128, 128])
    din("masku", [128, 128]); din("maskud", [128, 128])
    din("sigmask", [128, NCORES])
    din("wfc_sl", [FH, c['DSH']]); din("wpr_sl", [DM, c['FSH']])
    dram['out_span'] = nc.dram_tensor("out_span", [SPAN, DM], F32,
                                      kind="ExternalOutput").ap()

    dint("w1t_sl", [c['DSH'], FH], BF16); dint("w1t", [DM, FH], BF16)
    dint("w2t_sl", [c['FSH'], DM], BF16); dint("w2t", [FH, DM], BF16)
    dint("odeg_scr", [NH * NCH * 128, 128])
    dint("qhat_scr", [NH * NCH * 128, 128])
    dint("pd_loc", [NH * 2 * 128, 128])
    dint("pd_all", [NCORES * NH * 2 * 128, 128])
    dint("g_scr", [SPAN, DM])
    dint("xnl_scr", [NK * 128, PSPAN], BF16)
    dint("wo_hi_scr", [NK * DM, 128], BF16)
    dint("wo_lo_scr", [NK * DM, 128], BF16)
    dint("wp_scr_hi", [128, NH * 3 * DM], BF16)
    dint("wp_scr_lo", [128, NH * 3 * DM], BF16)
    dint("qkv_scr", [128, NH * 3 * SPAN])
    dint("oc_hi", [128, NH * SPAN], BF16)
    dint("oc_lo", [128, NH * SPAN], BF16)
    if 'xn_dbg' in dbg_outputs:
        dint("xn_dbg", [2 * NK * 128, PSPAN], BF16)

    import contextlib
    es = contextlib.ExitStack()
    with es:
        xn_hi = es.enter_context(nc.sbuf_tensor("xn_hi", [128, NK * PSPAN], BF16))
        beta_tok = es.enter_context(nc.sbuf_tensor("beta_tok", [128, NCH * NH], F32))
        cst = es.enter_context(nc.sbuf_tensor("cst", [128, 384 + 2 + 12 * NK + NCORES], F32))
        identb = es.enter_context(nc.sbuf_tensor("identb", [128, 128], BF16))
        af32_t = es.enter_context(nc.sbuf_tensor("af32", [128, AF32_N], F32))
        psum_all = es.enter_context(nc.psum_tensor("pball", [128, 4096], F32))
        psum = [psum_all[:, i * 512:(i + 1) * 512] for i in range(8)]

        T = dict(
            xn_hi=xn_hi, beta_tok=beta_tok,
            ident=cst[:, 0:128], masku=cst[:, 128:256], maskud=cst[:, 256:384],
            rmsw=cst[:, 384:385], onesc=cst[:, 385:386],
            convw=cst[:, 386:386 + 12 * NK],
            sigmask=cst[:, 386 + 12 * NK:386 + 12 * NK + NCORES],
            identb=identb[:, :],
            af32_t=af32_t,
            psum=psum, psum_all=psum_all[:, :], banks=[Bank(p) for p in psum])
        _ar = Arena(af32_t, AF32_N, 'ar')
        T['A'] = _ar
        T['AB'] = ArenaB(_ar)

        emit_all(P, c, dram, T, phases, dbg_outputs)

        sems = {}
        for s in P.semnames:
            sems[s] = es.enter_context(nc.semaphore(s))
        with nc.Block() as block:
            P.replay(block, sems, nc_scope=nc.named_scope)
    return nc, c


def emit_all(P, c, dram, T, phases=('p0', 'p1', 'p2', 'ag2', 'p34'), dbg_outputs=()):
    for name, ap in (("ident", T['ident']), ("masku", T['masku']),
                     ("maskud", T['maskud']), ("rmsw", T['rmsw']),
                     ("convw", T['convw']), ("sigmask", T['sigmask'])):
        P.dma('sync', ap, dram[name][:, :])
    cl = ('d_sync', P.v('d_sync'))
    P.vec(lambda e, S: e.tensor_copy(out=T['identb'], in_=T['ident']), wait=[cl])
    P.vec(lambda e, S: e.memset(T['onesc'], 1.0))

    if 'p0' in phases:
        emit_p0(P, c, dram, T)
    if 'p1' in phases:
        P.barrier()
        emit_p1(P, c, dram, T)
    if 'p2' in phases:
        P.barrier()
        emit_p2(P, c, dram, T)
    if 'ag2' in phases:
        P.sem('d_pd')
        P.emit('gpsimd', lambda e, S: e.collective_compute(
            "AllGather", OP.bypass, ins=[dram['pd_loc'][:, :].opt()],
            outs=[dram['pd_all'][:, :].opt()],
            replica_groups=[list(range(c['NCORES']))]),
            wait=[('d_pd', P.v('d_pd'))], inc=[('cc', 1)])
    if 'p34' in phases:
        P.barrier()
        emit_p34(P, c, dram, T)
    if 'xn_dbg' in dbg_outputs:
        NK, PSPAN = c['NK'], c['PSPAN']
        for k in range(NK):
            P.dma('sync', dram['xn_dbg'][k * 128:(k + 1) * 128, :],
                  T['xn_hi'][:, k * PSPAN:(k + 1) * PSPAN],
                  wait=[('t_v', P.v('t_v')), ('t_s', P.v('t_s'))])
            P.dma('sync', dram['xn_dbg'][(NK + k) * 128:(NK + k + 1) * 128, :],
                  T['xn_lo'][:, k * PSPAN:(k + 1) * PSPAN])
    if 'p4' not in phases and 'p34' not in phases:
        z = T['A'].alloc(c['DM'], 'zout')
        P.vec(lambda e, S: e.memset(z, 0.0))
        for t in range(c['NT']):
            P.dma('sync', dram['out_span'][t * 128:(t + 1) * 128, :], z,
                  wait=[('t_v', P.v('t_v'))])
    P.emit('sync', None, wait=[('d_sync', P.v('d_sync')),
                               ('d_scalar', P.v('d_scalar')),
                               ('d_gpsimd', P.v('d_gpsimd'))])



def host_prepare(cfg, ins):
    c = derive(cfg)
    B, S, DM, NH = c['B'], c['S'], c['DM'], c['NH']
    NCORES, SPAN, NK = c['NCORES'], c['SPAN'], c['NK']
    x = np.ascontiguousarray(ins['L0_attn_x'].reshape(B * S, DM))
    convw = np.zeros((128, 12 * NK), np.float32)
    for ti, name in enumerate(('conv_q', 'conv_k', 'conv_v')):
        cw = ins['L0_attn_' + name]
        for tap in range(4):
            for k in range(NK):
                convw[:, (ti * 4 + tap) * NK + k] = cw[k * 128:(k + 1) * 128, tap]
    ident = np.eye(128, dtype=np.float32)
    masku = (np.arange(128)[:, None] < np.arange(128)[None, :]).astype(np.float32)
    maskud = (np.arange(128)[:, None] <= np.arange(128)[None, :]).astype(np.float32)
    rmsw = np.ascontiguousarray(ins['L0_attn_rms_w'].reshape(128, 1))
    in_maps = []
    for core in range(NCORES):
        t0 = core * SPAN
        sstart = t0 % S
        halo = x[t0 - 3:t0] if sstart > 0 else np.zeros((3, DM), np.float32)
        first = (core // c['SPS']) * c['SPS']
        sm = np.zeros((128, NCORES), np.float32)
        for c2 in range(NCORES):
            if first <= c2 < core:
                sm[:, c2] = 1.0
        m = {
            'x_span': x[t0:t0 + SPAN],
            'x_halo': np.ascontiguousarray(halo),
            'wq': ins['L0_attn_Wq'], 'wk': ins['L0_attn_Wk'],
            'wv': ins['L0_attn_Wv'], 'wo': ins['L0_attn_Wo'],
            'wb': ins['L0_attn_Wb'], 'convw': convw, 'rmsw': rmsw,
            'ident': ident, 'masku': masku, 'maskud': maskud,
            'sigmask': sm,
            'wfc_sl': np.ascontiguousarray(
                ins['L0_ffn_W_fc'][:, core * c['DSH']:(core + 1) * c['DSH']]),
            'wpr_sl': np.ascontiguousarray(
                ins['L0_ffn_W_proj'][:, core * c['FSH']:(core + 1) * c['FSH']]),
        }
        in_maps.append(m)
    return c, in_maps






def emit_p1(P, c, dram, T):
    A = T['A']
    NK, PSPAN, NT, DM = c['NK'], c['PSPAN'], c['NT'], c['DM']
    banks = T['banks']
    P.sem('d_xl')
    A.push()
    xt = A.alloc(DM, 'xt')
    xsq = A.alloc(DM, 'xsq')
    st = A.alloc(8, 'st')
    xn_tok = A.alloc(DM, 'xn_tok')
    AB = T['AB']
    lost = AB.alloc(NK * 128, 'lost')

    def ln_tile(src_ap, nrows, dst_col, prev_dma):
        dw = P.dma('sync', xt[0:nrows, :], src_ap, wait=prev_dma)
        P.sca(lambda e, S, n=nrows: e.activation(
            out=xsq[0:n, :], in_=xt[0:n, :], func=AF.Square), wait=[dw])
        P.vec(lambda e, S, n=nrows: e.tensor_reduce(
            out=st[0:n, 0:1], in_=xt[0:n, :], op=OP.add, axis=AX.X), wait=[dw])
        P.vec(lambda e, S, n=nrows: e.tensor_reduce(
            out=st[0:n, 1:2], in_=xsq[0:n, :], op=OP.add, axis=AX.X),
            wait=[('t_s', P.v('t_s'))])
        P.vec(lambda e, S, n=nrows: e.tensor_scalar(
            out=st[0:n, 2:4], in0=st[0:n, 0:2], scalar1=float(np.float32(1.0 / c['DM'])),
            scalar2=None, op0=OP.mult))
        P.vec(lambda e, S, n=nrows: e.tensor_tensor(
            out=st[0:n, 4:5], in0=st[0:n, 2:3], in1=st[0:n, 2:3], op=OP.mult))
        P.vec(lambda e, S, n=nrows: e.scalar_tensor_tensor(
            out=st[0:n, 4:5], in0=st[0:n, 3:4], scalar=float(np.float32(1.0)),
            in1=st[0:n, 4:5], op0=OP.mult, op1=OP.subtract))
        P.vec(lambda e, S, n=nrows: e.tensor_scalar(
            out=st[0:n, 4:5], in0=st[0:n, 4:5], scalar1=float(np.float32(1e-5)),
            scalar2=None, op0=OP.add))
        P.sca(lambda e, S, n=nrows: e.activation(
            out=st[0:n, 5:6], in_=st[0:n, 4:5], func=AF.Sqrt),
            wait=[('t_v', P.v('t_v'))])
        P.vec(lambda e, S, n=nrows: e.reciprocal(
            out=st[0:n, 6:7], in_=st[0:n, 5:6]), wait=[('t_s', P.v('t_s'))])
        P.vec(lambda e, S, n=nrows: e.scalar_tensor_tensor(
            out=st[0:n, 7:8], in0=st[0:n, 2:3], scalar=float(np.float32(-1.0)),
            in1=st[0:n, 6:7], op0=OP.mult, op1=OP.mult))
        P.sca(lambda e, S, n=nrows: e.activation(
            out=xn_tok[0:n, :], in_=xt[0:n, :], func=AF.Identity,
            scale=st[0:n, 6:7], bias=st[0:n, 7:8]), wait=[('t_v', P.v('t_v'))])
        sv = ('t_s', P.v('t_s'))
        for k in range(NK):
            bank = banks[4 + (k % 4)]
            pv = bank.ap[:, 0:nrows]
            tv = P.ten(lambda e, S, p_=pv, s_=xn_tok[0:nrows, k * 128:(k + 1) * 128],
                       i_=T['ident'][0:nrows, 0:nrows]: e.transpose(p_, s_, i_),
                       wait=[sv] + bank.wwait())
            bank.w = tv
            hi_ap = T['xn_hi'][:, k * PSPAN + dst_col: k * PSPAN + dst_col + nrows]
            lo_ap = lost[:, k * 128: k * 128 + nrows]
            P.sca(lambda e, S, a=hi_ap, p_=pv: e.copy(a, p_), wait=[tv])
            rv = P.vec(lambda e, S, a=lo_ap, p_=pv, h=hi_ap: e.tensor_tensor(
                out=a, in0=p_, in1=h, op=OP.subtract),
                wait=[tv, ('t_s', P.v('t_s')), ('d_xl', P.v('d_xl'))])
            bank.r = rv
        P.dma('gpsimd',
              dram['xnl_scr'].rearrange("(k p) col -> p k col", p=128)[:, :, dst_col:dst_col + nrows],
              lost.rearrange("p (k col) -> p k col", col=128)[:, :, 0:nrows],
              wait=[('t_v', P.v('t_v'))], sem='d_xl')
        return [('t_v', P.v('t_v')), ('t_s', P.v('t_s'))]

    prev = ln_tile(dram['x_halo'][:, :], 3, 0, P.guards())
    for t in range(NT):
        prev = ln_tile(dram['x_span'][t * 128:(t + 1) * 128, :], 128,
                       3 + t * 128, prev)
    A.pop()




def emit_p0(P, c, dram, T):
    A, AB, banks = T['A'], T['AB'], T['banks']
    for which, Wsl, nrows, ncols, wTsl in (
            ('fc', dram['wfc_sl'], c['FH'], c['DSH'], dram['w1t_sl']),
            ('pr', dram['wpr_sl'], c['DM'], c['FSH'], dram['w2t_sl'])):
        A.push(); AB.push()
        NTI = nrows // 128
        G = ncols // 64
        NOC = ncols // 128
        P.sem('d_ccin')
        wt_f = A.alloc(ncols, 'wt_f')
        wabs = A.alloc(ncols, 'wabs')
        wab = AB.alloc(ncols, 'wab')
        sg = AB.alloc(ncols, 'sg')
        u = AB.alloc(ncols, 'u')
        m1 = AB.alloc(ncols, 'm1')
        w1tile = AB.alloc(ncols, 'w1tile')
        gs = A.alloc(G, 'gs')
        scb = AB.alloc(G, 'scb')
        tt = A.alloc(G, 'tt')
        tst = (AB.alloc(128, 'tst_a'), AB.alloc(128, 'tst_b'))
        for ti in range(NTI):
            dw = P.dma('sync', wt_f, Wsl[ti * 128:(ti + 1) * 128, :],
                       wait=[('t_v', P.v('t_v')), ('t_s', P.v('t_s'))])
            P.vec(lambda e, S, a=wabs.bitcast(I32), b=wt_f.bitcast(I32):
                e.tensor_scalar(out=a, in0=b, scalar1=0x7FFFFFFF, scalar2=None,
                op0=OP.bitwise_and), wait=[dw, ('d_ccin', P.v('d_ccin'))])
            P.vec(lambda e, S, a=wab, b=wabs: e.tensor_copy(out=a, in_=b))
            P.vec(lambda e, S, a=gs.rearrange("p (g o) -> p g o", o=1),
                b=wab.rearrange("p (g o) -> p g o", o=64):
                e.tensor_reduce(out=a, in_=b, op=OP.add, axis=AX.X))
            P.vec(lambda e, S, a=scb, b=gs: e.tensor_scalar(
                out=a, in0=b, scalar1=float(np.float32(1.0 / 64)), scalar2=None,
                op0=OP.mult))
            P.vec(lambda e, S, a=scb: e.tensor_scalar(
                out=a, in0=a, scalar1=float(np.float32(BF(1e-8))), scalar2=None,
                op0=OP.max))
            P.vec(lambda e, S, a=tt, b=scb: e.tensor_scalar(
                out=a, in0=b, scalar1=float(C_THR), scalar2=None, op0=OP.mult))
            P.vec(lambda e, S, a=u.rearrange("p (g o) -> p g o", o=64),
                b=wab.rearrange("p (g o) -> p g o", o=64),
                t_=tt.rearrange("p (g o) -> p g o", o=1).broadcast_to([128, G, 64]):
                e.tensor_tensor(out=a, in0=b, in1=t_, op=OP.is_gt))
            P.sca(lambda e, S, a=sg, b=wt_f: e.activation(out=a, in_=b, func=AF.Sign),
                  wait=[dw, ('d_ccin', P.v('d_ccin'))])
            P.vec(lambda e, S, a=m1.rearrange("p (g o) -> p g o", o=64),
                b=u.rearrange("p (g o) -> p g o", o=64),
                t_=scb.rearrange("p (g o) -> p g o", o=1).broadcast_to([128, G, 64]):
                e.tensor_tensor(out=a, in0=b, in1=t_, op=OP.mult))
            P.vec(lambda e, S, a=w1tile, b=m1, d=sg: e.tensor_tensor(
                out=a, in0=b, in1=d, op=OP.mult),
                wait=[('t_s', P.v('t_s'))])
            vv = ('t_v', P.v('t_v'))
            for oc in range(NOC):
                bank = banks[oc % 2]
                pv = bank.ap[:, 0:64].bitcast(BF16)
                tv = P.ten(lambda e, S, p_=pv,
                           s_=w1tile[:, oc * 128:(oc + 1) * 128]:
                           e.transpose(p_, s_, T['identb']),
                           wait=[vv] + bank.wwait())
                bank.w = tv
                sb = tst[oc % 2]
                bank.r = P.vec(lambda e, S, d=sb, p_=pv:
                               e.tensor_copy(out=d, in_=p_),
                               wait=[tv, ('d_ccin', P.v('d_ccin'))])
                P.dma('sync', wTsl[oc * 128:(oc + 1) * 128,
                                   ti * 128:(ti + 1) * 128], sb,
                      wait=[('t_v', P.v('t_v'))], sem='d_ccin')
        A.pop(); AB.pop()
    w = ('d_ccin', P.v('d_ccin'))
    P.emit('gpsimd', lambda e, S: e.collective_compute(
        "AllGather", OP.bypass, ins=[dram['w1t_sl'][:, :].opt()],
        outs=[dram['w1t'][:, :].opt()],
        replica_groups=[list(range(c['NCORES']))]), wait=[w], inc=[('cc', 1)])
    P.emit('gpsimd', lambda e, S: e.collective_compute(
        "AllGather", OP.bypass, ins=[dram['w2t_sl'][:, :].opt()],
        outs=[dram['w2t'][:, :].opt()],
        replica_groups=[list(range(c['NCORES']))]), wait=[w], inc=[('cc', 1)])



NI = 3


def emit_p2(P, c, dram, T):
    NH, NK, SPAN, PSPAN, NCH = c['NH'], c['NK'], c['SPAN'], c['PSPAN'], c['NCH']
    DM = c['DM']
    A, AB = T['A'], T['AB']
    banks = T['banks']
    xn_hi = T['xn_hi']
    ident, masku, maskud = T['ident'], T['masku'], T['maskud']
    beta_tok = T['beta_tok']
    NTOK = len(c['TOKSUBS'])
    for s in ('d_oq', 'd_pd', 'd_xlo0', 'd_xlo1', 'd_wst', 'd_wsl', 'd_qkv'):
        P.sem(s)
    for par in range(2):
        P.sem('d_t3_%d' % par)

    def tmm(bank, lhsT, rhs, start, stop, wait=(), width=512, rows=128, tag=''):
        ws = [x for x in wait if x]
        if start:
            ws += bank.wwait()
        return P.ten(lambda e, S, a_=bank.ap[0:rows, 0:width], l_=lhsT, r_=rhs,
                     st=start, sp=stop: e.matmul(a_, l_, r_, start=st, stop=sp),
                     wait=ws, tag=tag)

    def mark_w(bank):
        bank.w = ('t_t', P.v('t_t'))

    def evacv(bank, fn, wait=()):
        bank.r = P.vec(fn, wait=list(bank.rwait()) + list(wait))
        return bank.r

    def evacs(bank, fn, wait=()):
        bank.r = P.sca(fn, wait=list(bank.rwait()) + list(wait))
        return bank.r

    def transpose_to(bank, src, n=128, wait=()):
        ws = bank.wwait() + [x for x in wait if x]
        pv = bank.ap[:, 0:n]
        bank.w = P.ten(lambda e, S, p_=pv, s_=src, i_=ident[0:n, 0:n]:
                       e.transpose(p_, s_, i_), wait=ws)
        return pv

    A.push(); AB.push()
    wnat = A.alloc(DM, 'wnat')
    pre = A.alloc(PSPAN, 'pre')
    acc_a = A.alloc(512, 'acc_a'); acc_b = A.alloc(512, 'acc_b')
    silb = (A.alloc(512, 'silb_a'), A.alloc(512, 'silb_b'))
    xlo2 = (AB.alloc(NK * 512, 'xlo_a'), AB.alloc(NK * 512, 'xlo_b'))
    wph2 = (AB.alloc(NK * 128, 'wph_a'), AB.alloc(NK * 128, 'wph_b'))
    wpl2 = (AB.alloc(NK * 128, 'wpl_a'), AB.alloc(NK * 128, 'wpl_b'))
    woh = AB.alloc(128, 'woh'); wol = AB.alloc(128, 'wol')
    bneg_tok = A.alloc(NCH * NH, 'bneg_tok')
    sq_shared = A.alloc(128, 'sq_shared')

    def xh(k, col0, wdt):
        return xn_hi[:, k * PSPAN + col0: k * PSPAN + col0 + wdt]

    def xlo_dma(tc0, tw, buf):
        return P.dma('scalar',
                     xlo2[buf].rearrange("p (k f) -> p k f", f=512)[:, :, 0:tw],
                     dram['xnl_scr'].rearrange("(k p) col -> p k col", p=128)
                     [:, :, tc0:tc0 + tw],
                     wait=[('t_t', P.v('t_t')), ('d_xl', P.v('d_xl'))],
                     sem='d_xlo%d' % buf)

    def xl(k, tw, buf):
        return xlo2[buf][:, k * 512: k * 512 + tw]

    Wsrc = (dram['wq'], dram['wk'], dram['wv'])

    A.push()
    wnat2 = (A.alloc(DM, 'wn2a'), A.alloc(DM, 'wn2b'))
    for h in range(NH):
        for ti in range(3):
            i = h * 3 + ti
            dw = P.dma('sync', wnat2[i % 2], Wsrc[ti][h * 128:(h + 1) * 128, :],
                       wait=P.guards() if i == 0 else [('t_t', P.v('t_t'))])
            wstg = ('d_wst', P.v('d_wst'))
            for k in range(NK):
                bkw = banks[1 + (k % 2)]
                pv = transpose_to(bkw, wnat2[i % 2][:, k * 128:(k + 1) * 128],
                                  wait=[dw])
                hi_ap = wph2[0][:, k * 128:(k + 1) * 128]
                lo_ap = wpl2[0][:, k * 128:(k + 1) * 128]
                evacs(bkw, lambda e, S, a=hi_ap, p_=pv: e.copy(a, p_),
                      wait=[wstg])
                evacv(bkw, lambda e, S, a=lo_ap, p_=pv, h_=hi_ap:
                      e.tensor_tensor(out=a, in0=p_, in1=h_, op=OP.subtract),
                      wait=[('t_s', P.v('t_s')), wstg])
            P.dma('scalar', dram['wp_scr_hi'][:, i * DM:(i + 1) * DM], wph2[0],
                  wait=[('t_s', P.v('t_s')), ('t_v', P.v('t_v'))], sem='d_wst')
            P.dma('scalar', dram['wp_scr_lo'][:, i * DM:(i + 1) * DM], wpl2[0],
                  wait=[('t_v', P.v('t_v'))], sem='d_wst')
    A.pop()
    wst_total = ('d_wst', P.v('d_wst'))

    A.push()
    beta_fm = A.alloc(SPAN, 'beta_fm')
    wbt_hi = AB.alloc(NK * 16, 'wbt_hi'); wbt_lo = AB.alloc(NK * 16, 'wbt_lo')
    dw = P.dma('sync', wnat[0:NH, :], dram['wb'][:, :],
               wait=[('t_t', P.v('t_t')), ('t_v', P.v('t_v')),
                     ('t_s', P.v('t_s'))])
    for k in range(NK):
        bk_t = banks[1 + (k % 2)]
        pv = transpose_to(bk_t, wnat[0:NH, k * 128:(k + 1) * 128], n=NH, wait=[dw])
        hi_ap = wbt_hi[:, k * 16:k * 16 + NH]
        lo_ap = wbt_lo[:, k * 16:k * 16 + NH]
        evacs(bk_t, lambda e, S, a=hi_ap, p_=pv: e.copy(a, p_))
        evacv(bk_t, lambda e, S, a=lo_ap, p_=pv, h=hi_ap: e.tensor_tensor(
            out=a, in0=p_, in1=h, op=OP.subtract), wait=[('t_s', P.v('t_s'))])
    wb_done = [('t_v', P.v('t_v')), ('t_s', P.v('t_s'))]
    bk_b = banks[0]
    dxls = {0: xlo_dma(3 + c['TOKSUBS'][0][0], c['TOKSUBS'][0][1], 0)}
    for si, (tc0, tw) in enumerate(c['TOKSUBS']):
        bf = si % 2
        if si + 1 < len(c['TOKSUBS']):
            dxls[si + 1] = xlo_dma(3 + c['TOKSUBS'][si + 1][0],
                                   c['TOKSUBS'][si + 1][1], (si + 1) % 2)
        dxl = dxls[si]
        nmm = NK * 3
        i = 0
        for k in range(NK):
            for wap, xap in ((wbt_hi[:, k * 16:k * 16 + NH], xh(k, 3 + tc0, tw)),
                             (wbt_hi[:, k * 16:k * 16 + NH], xl(k, tw, bf)),
                             (wbt_lo[:, k * 16:k * 16 + NH], xh(k, 3 + tc0, tw))):
                tmm(bk_b, wap, xap, start=(i == 0), stop=(i == nmm - 1),
                    wait=(wb_done + [dxl]) if i == 0 else (), width=tw, rows=NH)
                i += 1
        mark_w(bk_b)
        evacs(bk_b, lambda e, S, a=beta_fm[0:NH, tc0:tc0 + tw],
              p_=bk_b.ap[0:NH, 0:tw]: e.activation(out=a, in_=p_, func=AF.Sigmoid))
    for j in range(NCH):
        pv = transpose_to(banks[1], beta_fm[0:NH, j * 128:(j + 1) * 128], n=NH,
                          wait=[('t_s', P.v('t_s'))])
        evacv(banks[1], lambda e, S, a=beta_tok[:, j * NH:(j + 1) * NH],
              p_=banks[1].ap[:, 0:NH]: e.tensor_copy(out=a, in_=p_))
    A.pop()
    P.vec(lambda e, S, a=bneg_tok[:, 0:NCH * NH], b=beta_tok[:, 0:NCH * NH]:
          e.tensor_scalar(out=a, in0=b, scalar1=-1.0, scalar2=None, op0=OP.mult))

    MH = 3
    tb3 = {}
    for nm in ('q_rt', 'k_rt', 'qn', 'q_fm', 'k_fm', 'gm', 'qpre', 'm1', 'sq',
               'bTV', 'y3', 'ost', 'qst'):
        tb3[nm] = A.alloc(MH * 128, nm + '3')
    tb3['nst'] = A.alloc(16, 'nst3')
    tb3['fpow'] = [A.alloc(MH * 128, f'fp3_{i}') for i in range(LMAX + 1)]
    tb3['ftc'] = (A.alloc(MH * 128, 'ftc3a'), A.alloc(MH * 128, 'ftc3b'))
    tb3['mb'] = (A.alloc(MH * 128, 'mb3a'), A.alloc(MH * 128, 'mb3b'))
    tb3['u_a'] = A.alloc(MH * 256, 'ua3')
    tb3['D'] = (A.alloc(MH * 128, 'D3a'), A.alloc(MH * 128, 'D3b'))
    tb3['P'] = (A.alloc(MH * 128, 'P3a'), A.alloc(MH * 128, 'P3b'))
    tb3['tile'] = [[A.alloc(MH * 128, f'tl3_{par}_{ci}') for ci in range(3)]
                   for par in range(2)]
    PSA = T['psum_all']

    class TrBank:
        def __init__(self, base):
            self.base = base
            self.w = ('t_t', P.v('t_t'))
            self.r = [('t_v', P.v('t_v')), ('t_s', P.v('t_s'))]

        def mmout(self, i, w):
            b = self.base + i
            return PSA[:, b * 512:b * 512 + w]

        def view(self, nh, w, off=0):
            return (PSA[:, self.base * 512:(self.base + nh) * 512]
                    .rearrange("p (b c) -> p b c", c=512)[:, :, off:off + w])

        def wwait(self):
            out = list(self.r) if isinstance(self.r, list) else ([self.r] if self.r else [])
            if self.w:
                out.append(self.w)
            return out

        def rwait(self):
            return [self.w] if self.w else []

    tA = TrBank(0)
    tB = TrBank(3)
    bP = (banks[6], banks[7])

    def proj_gen(h):
        seq = [(ti, s) for ti in range(3) for s in range(len(c['TSUBS']))]
        dxls = {}
        def pf(idx):
            if idx < len(seq) and idx not in dxls:
                ti_, s_ = seq[idx]
                tc0_, tw_ = c['TSUBS'][s_]
                dxls[idx] = xlo_dma(tc0_, tw_, idx % 2)
        pf(0)
        for ti in range(3):
            i = h * 3 + ti
            pb = i % 2
            whi, wlo = wph2[pb], wpl2[pb]
            P.dma('scalar', whi, dram['wp_scr_hi'][:, i * DM:(i + 1) * DM],
                  wait=[wst_total, ('t_t', P.v('t_t'))], sem='d_wsl')
            P.dma('scalar', wlo, dram['wp_scr_lo'][:, i * DM:(i + 1) * DM],
                  sem='d_wsl')
            slab_w = ('d_wsl', P.v('d_wsl'))
            yield
            for s_i, (tc0, tw) in enumerate(c['TSUBS']):
                idx = ti * len(c['TSUBS']) + s_i
                dxl = dxls[idx]
                pf(idx + 1)
                bkp = bP[s_i % 2]
                nmm = NK * 3
                i2 = 0
                for k in range(NK):
                    for wap, xap in ((whi[:, k * 128:(k + 1) * 128], xh(k, tc0, tw)),
                                     (whi[:, k * 128:(k + 1) * 128], xl(k, tw, idx % 2)),
                                     (wlo[:, k * 128:(k + 1) * 128], xh(k, tc0, tw))):
                        tmm(bkp, wap, xap, start=(i2 == 0), stop=(i2 == nmm - 1),
                            wait=[slab_w, dxl] if i2 == 0 else (), width=tw,
                            tag=f'pj-{h}-{ti}-{s_i}')
                        i2 += 1
                    if k % 2 == 1:
                        yield
                mark_w(bkp)
                evacv(bkp, lambda e, S, a=pre[:, tc0:tc0 + tw],
                      p_=bkp.ap[:, 0:tw]: e.tensor_copy(out=a, in_=p_))
                yield
            for s_i, (tc0, tw) in enumerate(c['TOKSUBS']):
                def cw(tap, _ti=ti):
                    i_ = (_ti * 4 + tap) * NK + h
                    return T['convw'][:, i_:i_ + 1]
                P.vec(lambda e, S, o=acc_a[:, 0:tw], i0=pre[:, tc0:tc0 + tw],
                      s_=cw(0): e.tensor_scalar(out=o, in0=i0, scalar1=s_,
                      scalar2=None, op0=OP.mult), wait=[('t_s', P.v('t_s'))])
                order = ((acc_a, acc_b), (acc_b, acc_a), (acc_a, acc_b))
                for tap in (1, 2, 3):
                    si_, di = order[tap - 1]
                    P.vec(lambda e, S, o=di[:, 0:tw],
                          i0=pre[:, tc0 + tap:tc0 + tap + tw], s_=cw(tap),
                          i1=si_[:, 0:tw]: e.scalar_tensor_tensor(
                              out=o, in0=i0, scalar=s_, in1=i1,
                              op0=OP.mult, op1=OP.add))
                sb = silb[s_i % 2]
                P.sca(lambda e, S, o=sb[:, 0:tw], i0=acc_b[:, 0:tw]:
                      e.activation(out=o, in_=i0, func=AF.Silu),
                      wait=[('t_v', P.v('t_v')), ('d_qkv', P.v('d_qkv'))])
                P.dma('gpsimd', dram['qkv_scr'][:, i * SPAN + tc0:i * SPAN + tc0 + tw],
                      sb[:, 0:tw], wait=[('t_s', P.v('t_s'))], sem='d_qkv')
                yield

    def bc1(ap_col, nh):
        return ap_col.rearrange("p (b o) -> p b o", o=1).broadcast_to([128, nh, 128])

    def bcm(m_ap, nh):
        return m_ap.rearrange("p (o c) -> p o c", o=1).broadcast_to([128, nh, 128])

    def r3(flat, nh, w=128):
        return flat.rearrange("p (b c) -> p b c", c=w)[:, 0:nh, :]

    def triple_gen(h0, nh):
        fpow, ftc = tb3['fpow'], tb3['ftc']
        pd_guard = ('d_pd', P.v('d_pd'))

        def mm3(tb, lhs_f, rhs_f, wait, width=128, tag='', pair_rhs=None):
            ws = list(wait) + tb.wwait()
            for i in range(nh):
                if pair_rhs is not None:
                    P.ten(lambda e, S, a_=tb.mmout(i, width),
                          l_=lhs_f(i), r_=rhs_f(i):
                          e.matmul(a_, l_, r_, start=True, stop=False),
                          wait=ws if i == 0 else (), tag=tag)
                    P.ten(lambda e, S, a_=tb.mmout(i, width),
                          l_=pair_rhs[0](i), r_=pair_rhs[1](i):
                          e.matmul(a_, l_, r_, start=False, stop=True))
                else:
                    P.ten(lambda e, S, a_=tb.mmout(i, width),
                          l_=lhs_f(i), r_=rhs_f(i):
                          e.matmul(a_, l_, r_, start=True, stop=True),
                          wait=ws if i == 0 else (), tag=tag)
            tb.w = ('t_t', P.v('t_t'))

        def tr3(tb, src_f, wait, tag=''):
            ws = list(wait) + tb.wwait()
            for i in range(nh):
                P.ten(lambda e, S, p_=tb.mmout(i, 128), s_=src_f(i),
                      i_=ident: e.transpose(p_, s_, i_),
                      wait=ws if i == 0 else (), tag=tag)
            tb.w = ('t_t', P.v('t_t'))

        def ev_v(tb, fn, wait=()):
            tb.r = P.vec(fn, wait=tb.rwait() + list(wait))
            return tb.r

        def ev_s(tb, fn, wait=()):
            tb.r = P.sca(fn, wait=tb.rwait() + list(wait))
            return tb.r

        def col(buf, i, w=128):
            return buf[:, i * w:(i + 1) * w]

        def tsidx(jj):
            tpos = jj * 128
            s = 0
            for si_, (tc0_, tw_) in enumerate(c['TOKSUBS']):
                if tc0_ <= tpos:
                    s = si_
            return s

        tile_w = {}

        def emit_tiles(jj):
            gate = ('d_qkv', 16 * ((3 * (h0 + nh - 1) + 2) * NTOK + tsidx(jj) + 1))
            for ci in range(3):
                P.dma('sync',
                      r3(tb3['tile'][jj % 2][ci], nh),
                      dram['qkv_scr'].rearrange("p (hh c) -> p hh c", c=3 * SPAN)
                      [:, h0:h0 + nh, ci * SPAN + jj * 128:ci * SPAN + (jj + 1) * 128],
                      wait=[gate, ('t_t', P.v('t_t'))] if ci == 0 else [gate],
                      sem='d_t3_%d' % (jj % 2))
            tile_w[jj] = ('d_t3_%d' % (jj % 2), P.v('d_t3_%d' % (jj % 2)))

        emit_tiles(0)
        for j in range(NCH):
            bslc = beta_tok[:, j * NH + h0: j * NH + h0 + nh]
            bnslc = bneg_tok[:, j * NH + h0: j * NH + h0 + nh]
            Dprev = tb3['D'][(j + 1) % 2]; Pprev = tb3['P'][(j + 1) % 2]
            Dnew = tb3['D'][j % 2]; Pnew = tb3['P'][j % 2]
            tg = ('t_t', P.v('t_t'))
            if j + 1 < NCH:
                emit_tiles(j + 1)
            tw_j = tile_w.pop(j)
            tl = tb3['tile'][j % 2]
            for ci, dst3 in ((0, r3(tb3['q_rt'], nh)),
                             (1, r3(tb3['k_rt'], nh)),
                             (2, r3(tb3['u_a'], nh, 256)[:, :, 0:128])):
                tr3(tA, lambda i, _t=tl[ci]: col(_t, i), wait=[tw_j], tag=f'tT-{h0}-{j}')
                ev_s(tA, lambda e, S, a=dst3, p_=tA.view(nh, 128):
                     e.copy(a, p_), wait=[tg])
                yield
            for (raw, ntok3, fm, is_q) in (
                    (tb3['q_rt'], r3(tb3['qn'], nh), tb3['q_fm'], True),
                    (tb3['k_rt'], r3(tb3['u_a'], nh, 256)[:, :, 128:256],
                     tb3['k_fm'], False)):
                P.vec(lambda e, S, a=tb3['sq'][:, 0:nh * 128],
                      b=raw[:, 0:nh * 128]: e.tensor_tensor(
                          out=a, in0=b, in1=b, op=OP.mult),
                      wait=[('t_s', P.v('t_s'))])
                P.vec(lambda e, S, a=tb3['nst'][:, 0:nh]
                      .rearrange("p (b o) -> p b o", o=1),
                      b=r3(tb3['sq'], nh): e.tensor_reduce(
                          out=a, in_=b, op=OP.add, axis=AX.X))
                P.vec(lambda e, S, a=tb3['nst'][:, 4:4 + nh],
                      b=tb3['nst'][:, 0:nh]: e.tensor_scalar(
                          out=a, in0=b, scalar1=1e-6, scalar2=None, op0=OP.add))
                yield
                P.sca(lambda e, S, a=tb3['nst'][:, 8:8 + nh],
                      b=tb3['nst'][:, 4:4 + nh]: e.activation(
                          out=a, in_=b, func=AF.Sqrt),
                      wait=[('t_v', P.v('t_v'))])
                P.vec(lambda e, S, a=tb3['nst'][:, 12:12 + nh],
                      b=tb3['nst'][:, 8:8 + nh]: e.reciprocal(out=a, in_=b),
                      wait=[('t_s', P.v('t_s'))])
                if is_q:
                    P.vec(lambda e, S, a=tb3['nst'][:, 12:12 + nh]:
                          e.tensor_scalar(out=a, in0=a, scalar1=HD_SC,
                                          scalar2=None, op0=OP.mult))
                P.vec(lambda e, S, a=ntok3, b=r3(raw, nh),
                      s_=bc1(tb3['nst'][:, 12:12 + nh], nh): e.tensor_tensor(
                          out=a, in0=b, in1=s_, op=OP.mult), wait=[tg])
                yield
                src3 = tb3['qn'] if is_q else None
                if is_q:
                    tr3(tA, lambda i: col(tb3['qn'], i),
                        wait=[('t_v', P.v('t_v'))], tag=f'nT-{h0}-{j}')
                else:
                    tr3(tA, lambda i: tb3['u_a'][:, i * 256 + 128:(i + 1) * 256],
                        wait=[('t_v', P.v('t_v'))], tag=f'nT-{h0}-{j}')
                ev_s(tA, lambda e, S, a=r3(fm, nh), p_=tA.view(nh, 128):
                     e.copy(a, p_))
                yield
            norm_done = [('t_s', P.v('t_s'))]
            mm3(tB, lambda i: col(tb3['k_fm'], i), lambda i: col(tb3['k_fm'], i),
                wait=norm_done, tag=f'KK-{h0}-{j}')
            ev_v(tB, lambda e, S, a=r3(fpow[0], nh), p_=tB.view(nh, 128),
                 b_=bc1(bnslc, nh): e.tensor_tensor(
                     out=a, in0=p_, in1=b_, op=OP.mult))
            P.vec(lambda e, S, a=r3(fpow[0], nh), m_=bcm(masku, nh):
                  e.tensor_tensor(out=a, in0=a, in1=m_, op=OP.mult))
            yield
            mm3(tB, lambda i: col(tb3['k_fm'], i), lambda i: col(tb3['q_fm'], i),
                wait=[('t_s', P.v('t_s'))], tag=f'G-{h0}-{j}')
            ev_v(tB, lambda e, S, a=r3(tb3['gm'], nh), p_=tB.view(nh, 128),
                 m_=bcm(maskud, nh): e.tensor_tensor(
                     out=a, in0=p_, in1=m_, op=OP.mult))
            yield
            tr3(tA, lambda i: col(fpow[0], i), wait=[('t_v', P.v('t_v'))],
                tag=f'fT-{h0}-{j}')
            ev_s(tA, lambda e, S, a=r3(ftc[0], nh), p_=tA.view(nh, 128):
                 e.copy(a, p_))
            mcur = tb3['mb'][0]
            tA.r = P.vec(lambda e, S, a=r3(mcur, nh), p_=tA.view(nh, 128),
                         i_=bcm(ident, nh): e.tensor_tensor(
                             out=a, in0=i_, in1=p_, op=OP.add),
                         wait=[('t_s', P.v('t_s'))])
            yield
            cur_ft = ftc[0]
            for l in range(1, LMAX + 1):
                pw = ('t_v', P.v('t_v'))
                sw = ('t_s', P.v('t_s'))
                mm3(tA, lambda i, _f=cur_ft: col(_f, i),
                    lambda i, _g=fpow[l - 1]: col(_g, i),
                    wait=[pw, sw], tag=f'sq-{h0}-{j}-{l}')
                ev_s(tA, lambda e, S, a=r3(fpow[l], nh), p_=tA.view(nh, 128):
                     e.copy(a, p_))
                mnxt = tb3['mb'][l % 2]
                mm3(tB, lambda i, _g=fpow[l]: col(_g, i),
                    lambda i, _m=mcur: col(_m, i),
                    wait=[('t_s', P.v('t_s')), pw], tag=f'M-{h0}-{j}-{l}')
                ev_v(tB, lambda e, S, a=r3(mnxt, nh), m_=r3(mcur, nh),
                     p_=tB.view(nh, 128): e.tensor_tensor(
                         out=a, in0=m_, in1=p_, op=OP.add))
                mcur = mnxt
                yield
                if l < LMAX:
                    nft = ftc[l % 2]
                    tr3(tA, lambda i, _g=fpow[l]: col(_g, i),
                        wait=[('t_s', P.v('t_s'))], tag=f'fT-{h0}-{j}-{l}')
                    ev_s(tA, lambda e, S, a=r3(nft, nh), p_=tA.view(nh, 128):
                         e.copy(a, p_))
                    cur_ft = nft
                    yield
            tm = tb3['k_rt']
            tr3(tA, lambda i, _m=mcur: col(_m, i), wait=[('t_v', P.v('t_v'))],
                tag=f'mT-{h0}-{j}')
            ev_s(tA, lambda e, S, a=r3(tm, nh), p_=tA.view(nh, 128): e.copy(a, p_))
            yield
            mm3(tB, lambda i: col(tm, i),
                lambda i: tb3['u_a'][:, i * 256:(i + 1) * 256],
                wait=[('t_s', P.v('t_s')), ('t_v', P.v('t_v'))], width=256,
                tag=f'ap-{h0}-{j}')
            ev_v(tB, lambda e, S, a=r3(tb3['bTV'], nh), p_=tB.view(nh, 128),
                 b_=bc1(bslc, nh): e.tensor_tensor(
                     out=a, in0=p_, in1=b_, op=OP.mult))
            tB.r = P.vec(lambda e, S, a=r3(tb3['y3'], nh),
                         p_=tB.view(nh, 128, off=128), b_=bc1(bslc, nh):
                         e.tensor_tensor(out=a, in0=p_, in1=b_, op=OP.mult))
            yield
            yt = tb3['qn']
            tr3(tA, lambda i: col(tb3['y3'], i), wait=[('t_v', P.v('t_v'))],
                tag=f'yT-{h0}-{j}')
            ev_s(tA, lambda e, S, a=r3(yt, nh), p_=tA.view(nh, 128): e.copy(a, p_))
            mm3(tB, lambda i: col(tb3['y3'], i),
                lambda i: tb3['u_a'][:, i * 256 + 128:(i + 1) * 256],
                wait=[('t_v', P.v('t_v'))], tag=f'C1-{h0}-{j}')
            ev_v(tB, lambda e, S, a=r3(tb3['m1'], nh), p_=tB.view(nh, 128),
                 i_=bcm(ident, nh): e.tensor_tensor(
                     out=a, in0=i_, in1=p_, op=OP.subtract))
            yield
            if j == 0:
                uhat = tb3['bTV']
            else:
                uhat = tb3['q_rt']
                mm3(tB, lambda i: col(yt, i), lambda i: col(Dprev, i),
                    wait=[('t_s', P.v('t_s')), ('t_v', P.v('t_v'))],
                    tag=f'YD-{h0}-{j}')
                ev_v(tB, lambda e, S, a=r3(uhat, nh), b_=r3(tb3['bTV'], nh),
                     p_=tB.view(nh, 128): e.tensor_tensor(
                         out=a, in0=b_, in1=p_, op=OP.subtract))
            mm3(tB, lambda i: col(tb3['y3'], i), lambda i: col(tb3['gm'], i),
                wait=[('t_v', P.v('t_v')), ('t_s', P.v('t_s'))],
                tag=f'YG-{h0}-{j}')
            ev_v(tB, lambda e, S, a=r3(tb3['qpre'], nh), q_=r3(tb3['q_fm'], nh),
                 p_=tB.view(nh, 128): e.tensor_tensor(
                     out=a, in0=q_, in1=p_, op=OP.subtract),
                 wait=[pd_guard] if j <= 1 else [])
            yield
            gv = [('t_v', P.v('t_v')), ('t_s', P.v('t_s'))]
            if j == 0:
                mm3(tB, lambda i: col(tb3['gm'], i), lambda i: col(uhat, i),
                    wait=gv, tag=f'O-{h0}-{j}')
            else:
                mm3(tB, lambda i: col(tb3['q_fm'], i), lambda i: col(Dprev, i),
                    wait=gv, tag=f'O-{h0}-{j}',
                    pair_rhs=(lambda i: col(tb3['gm'], i), lambda i: col(uhat, i)))
            ev_s(tB, lambda e, S, a=r3(tb3['ost'], nh), p_=tB.view(nh, 128):
                 e.copy(a, p_), wait=[('d_oq', P.v('d_oq'))])
            mm3(tB, lambda i: tb3['u_a'][:, i * 256 + 128:(i + 1) * 256],
                lambda i: col(uhat, i), wait=gv, tag=f'Du-{h0}-{j}')
            g_ = [pd_guard] if j <= 1 else []
            if j == 0:
                ev_s(tB, lambda e, S, a=r3(Dnew, nh), p_=tB.view(nh, 128):
                     e.copy(a, p_), wait=g_)
            else:
                ev_v(tB, lambda e, S, a=r3(Dnew, nh), d_=r3(Dprev, nh),
                     p_=tB.view(nh, 128): e.tensor_tensor(
                         out=a, in0=d_, in1=p_, op=OP.add), wait=g_)
            yield
            if j == 0:
                P.sca(lambda e, S, a=tb3['qst'][:, 0:nh * 128],
                      b=tb3['qpre'][:, 0:nh * 128]: e.copy(a, b),
                      wait=[('d_oq', P.v('d_oq')), ('t_v', P.v('t_v'))])
            else:
                mm3(tA, lambda i: col(Pprev, i), lambda i: col(tb3['qpre'], i),
                    wait=[('t_v', P.v('t_v'))], tag=f'Qh-{h0}-{j}')
                ev_s(tA, lambda e, S, a=r3(tb3['qst'], nh), p_=tA.view(nh, 128):
                     e.copy(a, p_), wait=[('d_oq', P.v('d_oq'))])
            if j == 0:
                mm3(tA, lambda i: col(tb3['m1'], i), lambda i: ident,
                    wait=[('t_v', P.v('t_v'))], tag=f'Pu-{h0}-{j}')
            else:
                mm3(tA, lambda i: col(tb3['m1'], i), lambda i: col(Pprev, i),
                    wait=[('t_v', P.v('t_v'))], tag=f'Pu-{h0}-{j}')
            ev_s(tA, lambda e, S, a=r3(Pnew, nh), p_=tA.view(nh, 128):
                 e.copy(a, p_), wait=g_)
            for i in range(nh):
                h = h0 + i
                ow = dram['odeg_scr'][(h * NCH + j) * 128:(h * NCH + j + 1) * 128, :]
                qw = dram['qhat_scr'][(h * NCH + j) * 128:(h * NCH + j + 1) * 128, :]
                P.dma('gpsimd', ow, col(tb3['ost'], i),
                      wait=[('t_s', P.v('t_s'))] if i == 0 else (), sem='d_oq')
                P.dma('gpsimd', qw, col(tb3['qst'], i), sem='d_oq')
            yield
        Pfin = tb3['P'][(NCH - 1) % 2]; Dfin = tb3['D'][(NCH - 1) % 2]
        tr3(tA, lambda i: col(Pfin, i),
            wait=[('t_v', P.v('t_v')), ('t_s', P.v('t_s')),
                  ('d_pd', P.v('d_pd'))], tag=f'pT-{h0}')
        ev_v(tA, lambda e, S, a=r3(tb3['qpre'], nh), p_=tA.view(nh, 128):
             e.tensor_copy(out=a, in_=p_))
        for i in range(nh):
            h = h0 + i
            P.dma('gpsimd', dram['pd_loc'][(h * 2) * 128:(h * 2 + 1) * 128, :],
                  col(tb3['qpre'], i),
                  wait=[('t_v', P.v('t_v'))] if i == 0 else (), sem='d_pd')
            P.dma('gpsimd', dram['pd_loc'][(h * 2 + 1) * 128:(h * 2 + 2) * 128, :],
                  col(Dfin, i), sem='d_pd')

    def chain(*gs):
        for g_ in gs:
            yield from g_

    def run_rr(gens):
        live = list(gens)
        while live:
            nxt = []
            for g in live:
                try:
                    next(g)
                    nxt.append(g)
                except StopIteration:
                    pass
            live = nxt

    def wo_gen():
        for co in range(NK):
            dw = P.dma('sync', wnat, dram['wo'][co * 128:(co + 1) * 128, :],
                       wait=[('t_t', P.v('t_t')), ('t_s', P.v('t_s'))])
            for k in range(NK):
                bkw = banks[6 + (k % 2)]
                pv = transpose_to(bkw, wnat[:, k * 128:(k + 1) * 128],
                                  wait=[dw, ('d_scalar', P.v('d_scalar'))])
                evacs(bkw, lambda e, S, p_=pv: e.activation(
                    out=woh, in_=p_, func=AF.Identity, scale=T['rmsw']),
                    wait=[('d_scalar', P.v('d_scalar'))])
                evacv(bkw, lambda e, S, p_=pv: e.scalar_tensor_tensor(
                    out=wol, in0=p_, scalar=T['rmsw'], in1=woh,
                    op0=OP.mult, op1=OP.subtract),
                    wait=[('t_s', P.v('t_s')), ('d_scalar', P.v('d_scalar'))])
                r = (co * NK + k) * 128
                P.dma('scalar', dram['wo_hi_scr'][r:r + 128, :], woh,
                      wait=[('t_s', P.v('t_s'))])
                P.dma('scalar', dram['wo_lo_scr'][r:r + 128, :], wol,
                      wait=[('t_v', P.v('t_v'))])
                if k % 4 == 3:
                    yield

    run_rr([chain(*[proj_gen(h2) for h2 in range(0, min(NI, NH))])])
    for g0 in range(0, NH, NI):
        nh = min(NI, NH - g0)
        gens = [triple_gen(g0, nh)]
        nxt_heads = list(range(g0 + NI, min(g0 + 2 * NI, NH)))
        if nxt_heads:
            gens.append(chain(*[proj_gen(h2) for h2 in nxt_heads]))
        if g0 + NI >= NH:
            gens.append(wo_gen())
        run_rr(gens)
    A.pop(); AB.pop()




def emit_p34(P, c, dram, T):
    NK, DM, FH, SPAN, PSPAN = c['NK'], c['DM'], c['FH'], c['SPAN'], c['PSPAN']
    NH, NCH, NCORES = c['NH'], c['NCH'], c['NCORES']
    HSUP = min(4, c['FHC'])
    NSUP = c['FHC'] // HSUP
    A, AB, banks = T['A'], T['AB'], T['banks']
    xn_hi = T['xn_hi']
    ident = T['ident']
    eps = float(np.float32(np.finfo(np.float32).eps))
    for s_ in ('d_xl', 'd_xlo3', 'd_w1a', 'd_w1b', 'd_w2a', 'd_w2b', 'd_g',
               'd_pda', 'd_oq2', 'd_oc', 'd_os0', 'd_os1', 'd_wo', 'd_xg', 'd_out'):
        P.sem(s_)

    def tmm(bank, lhsT, rhs, start, stop, wait=(), width=512, rows=128, tag=''):
        ws = [x for x in wait if x]
        if start:
            ws += bank.wwait()
        return P.ten(lambda e, S, a_=bank.ap[0:rows, 0:width], l_=lhsT, r_=rhs,
                     st=start, sp=stop: e.matmul(a_, l_, r_, start=st, stop=sp),
                     wait=ws, tag=tag)

    def mark_w(bank):
        bank.w = ('t_t', P.v('t_t'))

    def evacv(bank, fn, wait=()):
        bank.r = P.vec(fn, wait=list(bank.rwait()) + list(wait))
        return bank.r

    def evacs(bank, fn, wait=()):
        bank.r = P.sca(fn, wait=list(bank.rwait()) + list(wait))
        return bank.r

    def transpose_to(bank, src, n=128, wait=()):
        ws = bank.wwait() + [x for x in wait if x]
        pv = bank.ap[:, 0:n]
        bank.w = P.ten(lambda e, S, p_=pv, s_=src, i_=ident[0:n, 0:n]:
                       e.transpose(p_, s_, i_), wait=ws)
        return pv

    A.push(); AB.push()
    w1b2 = (AB.alloc(NK * HSUP * 128, 'w1a'), AB.alloc(NK * HSUP * 128, 'w1b'))
    xlo3 = AB.alloc(NK * 512, 'xlo3')
    w2b2 = (AB.alloc(HSUP * DM // 2, 'w2a'), AB.alloc(HSUP * DM // 2, 'w2b'))
    yhi = AB.alloc(HSUP * 512, 'yhi')
    ylo = AB.alloc(HSUP * 512, 'ylo')
    rbuf2 = (A.alloc(512, 'rbuf_a'), A.alloc(512, 'rbuf_b'))
    ybuf = A.alloc(512, 'ybuf')
    zbuf = A.alloc(512, 'zbuf')
    out_acc = A.alloc(NK * 512, 'out_acc')
    sr = A.alloc(512, 'sr')
    rtok = A.alloc(8, 'rtok')
    gbuf = A.alloc(DM, 'gbuf')
    HHALF = min(4, NH)
    pdb = A.alloc(HHALF * 2 * 128, 'pdb')
    sig_a = A.alloc(NH * 128, 'sig_a')
    stmp = A.alloc(128, 'stmp')
    ob = A.alloc(256, 'ob')
    qb = A.alloc(256, 'qb')
    orw = A.alloc(128, 'orw')
    sqt = A.alloc(128, 'sqt')
    nst = A.alloc(8, 'nst')
    onr = A.alloc(128, 'onr')
    ohi_st = AB.alloc(128, 'ohi_st')
    olo_st = AB.alloc(128, 'olo_st')

    def ffn_gen():
        bh2 = (banks[0], banks[1])
        bs, bo, bt = banks[7], banks[2], banks[3]
        cc_wait = ('cc', P.v('cc'))
        entry_v = ('t_v', P.v('t_v'))
        entry_s = ('t_s', P.v('t_s'))
        for (tc0, tw) in c['TOKSUBS']:
            dxl = P.dma('gpsimd',
                        xlo3.rearrange("p (k f) -> p k f", f=512)[:, :, 0:tw],
                        dram['xnl_scr'].rearrange("(k p) col -> p k col", p=128)
                        [:, :, 3 + tc0:3 + tc0 + tw],
                        wait=[('t_t', P.v('t_t')), ('d_xl', P.v('d_xl')),
                              entry_v, entry_s],
                        sem='d_xlo3')
            ss_open = False
            HW = HSUP * 128
            DH = DM // 2
            wdma = {}
            wdma2 = {}

            def wpf(hs_):
                if hs_ < NSUP and hs_ not in wdma:
                    wdma[hs_] = P.dma('sync',
                              w1b2[hs_ % 2].rearrange("p (k f) -> p k f", f=HW),
                              dram['w1t'].rearrange("(k p) f -> p k f", p=128)
                              [:, :, hs_ * HW:(hs_ + 1) * HW],
                              wait=[cc_wait, entry_v, entry_s,
                                    ('t_t', P.v('t_t'))],
                              sem='d_w1a' if hs_ % 2 == 0 else 'd_w1b')

            def wpf2(hs_, half):
                if hs_ < NSUP and (hs_, half) not in wdma2:
                    wdma2[(hs_, half)] = P.dma('scalar',
                              w2b2[half].rearrange("p (i f) -> p i f", f=DH),
                              dram['w2t'][hs_ * HSUP * 128:(hs_ + 1) * HSUP * 128,
                                          half * DH:(half + 1) * DH]
                              .rearrange("(i p) f -> p i f", p=128),
                              wait=[cc_wait, entry_v, entry_s,
                                    ('t_t', P.v('t_t'))],
                              sem='d_w2a' if half == 0 else 'd_w2b')
            wpf(0)
            wpf2(0, 0)
            for hs in range(NSUP):
                wall = wdma[hs]
                wpf(hs + 1)
                wpf2(hs, 1)
                w1buf = w1b2[hs % 2]
                for i in range(HSUP):
                    bh = bh2[i % 2]
                    rbuf = rbuf2[i % 2]
                    nmm = NK * 2
                    im = 0
                    for k in range(NK):
                        wap = w1buf[:, k * HSUP * 128 + i * 128:
                                    k * HSUP * 128 + (i + 1) * 128]
                        for hx in (1, 0):
                            xap = (xn_hi[:, k * PSPAN + 3 + tc0:k * PSPAN + 3 + tc0 + tw]
                                   if hx else xlo3[:, k * 512:k * 512 + tw])
                            tmm(bh, wap, xap, start=(im == 0), stop=(im == nmm - 1),
                                wait=[wall, dxl] if im == 0 else (), width=tw,
                                tag=f'fc-t{tc0}-{hs}-{i}')
                            im += 1
                    mark_w(bh)
                    bh.r = P.sca(lambda e, S, a=rbuf[:, 0:tw], p_=bh.ap[:, 0:tw]:
                                 e.activation(out=a, in_=p_, func=AF.Relu),
                                 wait=bh.rwait())
                    P.vec(lambda e, S, a=ybuf[:, 0:tw], b=rbuf[:, 0:tw]:
                          e.tensor_tensor(out=a, in0=b, in1=b, op=OP.mult),
                          wait=[('t_s', P.v('t_s'))])
                    P.sca(lambda e, S, a=zbuf[:, 0:tw], b=ybuf[:, 0:tw]:
                          e.activation(out=a, in_=b, func=AF.Square),
                          wait=[('t_v', P.v('t_v'))])
                    P.vec(lambda e, S, a=yhi[:, i * 512:i * 512 + tw], b=ybuf[:, 0:tw]:
                          e.tensor_copy(out=a, in_=b))
                    P.vec(lambda e, S, a=ylo[:, i * 512:i * 512 + tw], b=ybuf[:, 0:tw],
                          h_=yhi[:, i * 512:i * 512 + tw]: e.tensor_tensor(
                              out=a, in0=b, in1=h_, op=OP.subtract))
                    tmm(bs, T['onesc'], zbuf[:, 0:tw], start=(not ss_open),
                        stop=(hs == NSUP - 1 and i == HSUP - 1),
                        wait=[('t_s', P.v('t_s'))], width=tw, rows=1,
                        tag=f'ss-t{tc0}-{hs}-{i}')
                    ss_open = True
                    mark_w(bs)
                    yield
                ydone = ('t_v', P.v('t_v'))
                for dk in range(NK):
                    half = (dk * 128) // DH
                    w2buf = w2b2[half]
                    whalf = wdma2[(hs, half)]
                    if dk * 128 == DH:
                        wpf2(hs + 1, 0)
                    nmm = HSUP * 2
                    im = 0
                    for i in range(HSUP):
                        dko = dk * 128 - half * DH
                        wap = w2buf[:, i * DH + dko:i * DH + dko + 128]
                        for yy in (yhi, ylo):
                            tmm(bo, wap, yy[:, i * 512:i * 512 + tw],
                                start=(im == 0), stop=(im == nmm - 1),
                                wait=[ydone, whalf] if im == 0 else (), width=tw,
                                tag=f'pr-t{tc0}-{hs}-{dk}')
                            im += 1
                    mark_w(bo)
                    if hs == 0:
                        bo.r = P.vec(lambda e, S, a=out_acc[:, dk * 512:dk * 512 + tw],
                                     p_=bo.ap[:, 0:tw]: e.tensor_copy(out=a, in_=p_),
                                     wait=list(bo.rwait()) + [('t_t', P.v('t_t'))])
                    else:
                        bo.r = P.vec(lambda e, S, a=out_acc[:, dk * 512:dk * 512 + tw],
                                     p_=bo.ap[:, 0:tw]: e.tensor_tensor(
                                         out=a, in0=a, in1=p_, op=OP.add),
                                     wait=bo.rwait())
                    if dk % 4 == 3:
                        yield
            bs.r = P.vec(lambda e, S, a=sr[0:1, 0:tw], p_=bs.ap[0:1, 0:tw]:
                         e.tensor_scalar(out=a, in0=p_,
                                         scalar1=float(np.float32(1.0 / FH)),
                                         scalar2=eps, op0=OP.mult, op1=OP.add),
                         wait=bs.rwait())
            P.sca(lambda e, S, a=sr[0:1, 0:tw]: e.activation(
                out=a, in_=a, func=AF.Sqrt), wait=[('t_v', P.v('t_v'))])
            P.vec(lambda e, S, a=sr[0:1, 0:tw]: e.reciprocal(
                out=a, in_=a), wait=[('t_s', P.v('t_s'))])
            rdone = ('t_v', P.v('t_v'))
            for tg in range(tw // 128):
                pv = bt.ap[:, 0:1]
                bt.w = P.ten(lambda e, S, p_=pv, s_=sr[0:1, tg * 128:(tg + 1) * 128],
                             i_=T['ident'][0:1, 0:1]: e.transpose(p_, s_, i_),
                             wait=bt.wwait() + [rdone], tag=f'rT-t{tc0}-{tg}')
                bt.r = P.vec(lambda e, S, a=rtok[:, 0:1], p_=pv:
                             e.tensor_copy(out=a, in_=p_),
                             wait=list(bt.rwait()) + [('t_s', P.v('t_s')),
                                                      ('d_g', P.v('d_g'))])
                for dk in range(NK):
                    pv2 = bt.ap[:, 4:132]
                    bt.w = P.ten(lambda e, S, p_=pv2,
                                 s_=out_acc[:, dk * 512 + tg * 128:dk * 512 + (tg + 1) * 128],
                                 i_=T['ident']: e.transpose(p_, s_, i_),
                                 wait=bt.wwait() + [('t_v', P.v('t_v'))],
                                 tag=f'oT-t{tc0}-{tg}-{dk}')
                    bt.r = P.vec(lambda e, S, a=gbuf[:, dk * 128:(dk + 1) * 128],
                                 p_=pv2: e.tensor_scalar(
                                     out=a, in0=p_, scalar1=rtok[:, 0:1],
                                     scalar2=None, op0=OP.mult),
                                 wait=bt.rwait())
                P.sca(lambda e, S, a=gbuf: e.activation(out=a, in_=a, func=AF.Silu),
                      wait=[('t_v', P.v('t_v')), ('d_g', P.v('d_g'))])
                P.dma('gpsimd', dram['g_scr'][tc0 + tg * 128:tc0 + (tg + 1) * 128, :],
                      gbuf, wait=[('t_s', P.v('t_s'))], sem='d_g')
                yield

    def p4ab_gen():
        b0, b1, b2 = banks[5], banks[4], banks[6]
        P.vec(lambda e, S: e.memset(sig_a, 0.0))
        for c2 in range(NCORES):
            m_ap = T['sigmask'][:, c2:c2 + 1]
            for hh in range(NH // HHALF):
                r0 = (c2 * NH + hh * HHALF) * 2 * 128
                dwp = P.dma('gpsimd',
                            pdb.rearrange("p (b d) -> p b d", d=128),
                            dram['pd_all'][r0:r0 + HHALF * 2 * 128, :]
                            .rearrange("(b p) d -> p b d", p=128),
                            wait=[('cc', P.v('cc')), ('t_v', P.v('t_v'))],
                            sem='d_pda')
                for hl in range(HHALF):
                    h = hh * HHALF + hl
                    pt = pdb[:, (hl * 2) * 128:(hl * 2 + 1) * 128]
                    dd = pdb[:, (hl * 2 + 1) * 128:(hl * 2 + 2) * 128]
                    ss = sig_a[:, h * 128:(h + 1) * 128]
                    tmm(b0, pt, ss, start=True, stop=True,
                        wait=[dwp, ('t_v', P.v('t_v'))], width=128,
                        tag=f'PS-{c2}-{h}')
                    mark_w(b0)
                    b0.r = P.vec(lambda e, S, a=orw, d_=dd, p_=b0.ap[:, 0:128]:
                                 e.tensor_tensor(out=a, in0=d_, in1=p_, op=OP.add),
                                 wait=b0.rwait())
                    P.vec(lambda e, S, a=orw, s_=ss: e.tensor_tensor(
                        out=a, in0=a, in1=s_, op=OP.subtract))
                    P.vec(lambda e, S, a=stmp, t_=orw, m_=m_ap, s_=ss:
                          e.scalar_tensor_tensor(out=a, in0=t_, scalar=m_, in1=s_,
                                                 op0=OP.mult, op1=OP.add))
                    P.vec(lambda e, S, a=ss: e.tensor_copy(out=a, in_=stmp))
                    yield
        sig_done = ('t_v', P.v('t_v'))
        for sp in range(SPAN // 256):
            for h in range(NH):
                r0 = (h * NCH + sp * 2) * 128
                P.dma('gpsimd', ob.rearrange("p (b d) -> p b d", d=128),
                      dram['odeg_scr'][r0:r0 + 256, :]
                      .rearrange("(b p) d -> p b d", p=128),
                      wait=[('t_v', P.v('t_v')), ('t_s', P.v('t_s')),
                            ('t_t', P.v('t_t'))], sem='d_oq2')
                P.dma('gpsimd', qb.rearrange("p (b d) -> p b d", d=128),
                      dram['qhat_scr'][r0:r0 + 256, :]
                      .rearrange("(b p) d -> p b d", p=128), sem='d_oq2')
                oq2w = ('d_oq2', P.v('d_oq2'))
                for jj in range(2):
                    tmm(b1, qb[:, jj * 128:(jj + 1) * 128],
                        sig_a[:, h * 128:(h + 1) * 128], start=True, stop=True,
                        wait=[oq2w, sig_done], width=128, tag=f'corr-{sp}-{h}-{jj}')
                    mark_w(b1)
                    b1.r = P.vec(lambda e, S, a=orw, o_=ob[:, jj * 128:(jj + 1) * 128],
                                 p_=b1.ap[:, 0:128]: e.tensor_tensor(
                                     out=a, in0=o_, in1=p_, op=OP.add),
                                 wait=list(b1.rwait()) + [oq2w])
                    P.vec(lambda e, S: e.tensor_tensor(
                        out=sqt, in0=orw, in1=orw, op=OP.mult))
                    P.vec(lambda e, S: e.tensor_reduce(
                        out=nst[:, 0:1], in_=sqt, op=OP.add, axis=AX.X))
                    P.vec(lambda e, S: e.tensor_scalar(
                        out=nst[:, 1:2], in0=nst[:, 0:1],
                        scalar1=float(np.float32(1.0 / 128)), scalar2=1e-5,
                        op0=OP.mult, op1=OP.add))
                    P.sca(lambda e, S: e.activation(
                        out=nst[:, 2:3], in_=nst[:, 1:2], func=AF.Sqrt),
                        wait=[('t_v', P.v('t_v'))])
                    P.vec(lambda e, S: e.reciprocal(out=nst[:, 3:4], in_=nst[:, 2:3]),
                          wait=[('t_s', P.v('t_s'))])
                    P.vec(lambda e, S: e.tensor_scalar(
                        out=onr, in0=orw, scalar1=nst[:, 3:4], scalar2=None,
                        op0=OP.mult))
                    pv = b2.ap[:, 0:128]
                    b2.w = P.ten(lambda e, S, p_=pv: e.transpose(p_, onr, ident),
                                 wait=b2.wwait() + [('t_v', P.v('t_v'))],
                                 tag=f'ocT-{sp}-{h}-{jj}')
                    b2.r = P.sca(lambda e, S, a=ohi_st, p_=pv: e.copy(a, p_),
                                 wait=list(b2.rwait()) + [('d_oc', P.v('d_oc'))])
                    b2.r = P.vec(lambda e, S, a=olo_st, p_=pv, h_=ohi_st:
                                 e.tensor_tensor(out=a, in0=p_, in1=h_,
                                                 op=OP.subtract),
                                 wait=list(b2.rwait()) + [('t_s', P.v('t_s')),
                                                          ('d_oc', P.v('d_oc'))])
                    cbase = h * SPAN + sp * 256 + jj * 128
                    P.dma('gpsimd', dram['oc_hi'][:, cbase:cbase + 128], ohi_st,
                          wait=[('t_s', P.v('t_s')), ('t_v', P.v('t_v'))],
                          sem='d_oc')
                    P.dma('gpsimd', dram['oc_lo'][:, cbase:cbase + 128], olo_st,
                          wait=[('t_v', P.v('t_v'))], sem='d_oc')
                    yield

    import os as _os
    if _os.environ.get('P34_SEQ'):
        for g in (ffn_gen(), p4ab_gen()):
            for _ in g:
                pass
    else:
        gens = [ffn_gen(), p4ab_gen()]
        live = list(gens)
        while live:
            nxt = []
            for g in live:
                try:
                    next(g)
                    nxt.append(g)
                except StopIteration:
                    pass
            live = nxt
    A.pop(); AB.pop()
    oc_done = ('d_oc', P.v('d_oc'))
    g_done = ('d_g', P.v('d_g'))

    A.push(); AB.push()
    ohi_sl2 = (AB.alloc(NK * 512, 'ohi_sla'), AB.alloc(NK * 512, 'ohi_slb'))
    olo_sl2 = (AB.alloc(NK * 512, 'olo_sla'), AB.alloc(NK * 512, 'olo_slb'))
    KH = max(1, NK // 2)
    woh = AB.alloc(KH * 128, 'woh')
    wol = AB.alloc(KH * 128, 'wol')
    sfmb = A.alloc(512, 'sfmb')
    s_tok = [A.alloc(DM, f's_tok{tg}') for tg in range(4)]
    gt = A.alloc(512, 'gt')
    xt = A.alloc(512, 'xt')
    obuf = A.alloc(512, 'obuf')
    bw, bts = banks[0], banks[3]
    osl = {}

    def osl_pf(s2):
        if s2 < SPAN // 512 and s2 not in osl:
            sm = 'd_os%d' % (s2 % 2)
            P.dma('sync', ohi_sl2[s2 % 2].rearrange("p (k f) -> p k f", f=512),
                  dram['oc_hi'].rearrange("p (k f) -> p k f", f=SPAN)
                  [:, :, s2 * 512:(s2 + 1) * 512],
                  wait=[oc_done, ('t_t', P.v('t_t'))], sem=sm)
            P.dma('scalar', olo_sl2[s2 % 2].rearrange("p (k f) -> p k f", f=512),
                  dram['oc_lo'].rearrange("p (k f) -> p k f", f=SPAN)
                  [:, :, s2 * 512:(s2 + 1) * 512],
                  wait=[oc_done, ('t_t', P.v('t_t'))], sem=sm)
            osl[s2] = (sm, P.v(sm))
    osl_pf(0)
    for sp2 in range(SPAN // 512):
        ohi_sl = ohi_sl2[sp2 % 2]
        olo_sl = olo_sl2[sp2 % 2]
        osw = osl.pop(sp2)
        osl_pf(sp2 + 1)
        for co in range(NK):
            nmm = NK * 3
            im = 0
            for kh in range(NK // KH):
                r0 = (co * NK + kh * KH) * 128
                P.dma('sync', woh.rearrange("p (b d) -> p b d", d=128),
                      dram['wo_hi_scr'][r0:r0 + KH * 128, :]
                      .rearrange("(b p) d -> p b d", p=128),
                      wait=[('t_t', P.v('t_t'))], sem='d_wo')
                P.dma('scalar', wol.rearrange("p (b d) -> p b d", d=128),
                      dram['wo_lo_scr'][r0:r0 + KH * 128, :]
                      .rearrange("(b p) d -> p b d", p=128),
                      wait=[('t_t', P.v('t_t'))], sem='d_wo')
                wow = ('d_wo', P.v('d_wo'))
                for kl in range(KH):
                    k = kh * KH + kl
                    for wch, och in ((woh, ohi_sl), (woh, olo_sl), (wol, ohi_sl)):
                        tmm(bw, wch[:, kl * 128:(kl + 1) * 128],
                            och[:, k * 512:(k + 1) * 512],
                            start=(im == 0), stop=(im == nmm - 1),
                            wait=([osw, wow]) if im == 0 else [wow] if kl == 0 else (),
                            width=512, tag=f'wo-{sp2}-{co}')
                        im += 1
            mark_w(bw)
            for tg in range(4):
                bw.r = P.vec(lambda e, S, a=sfmb[:, tg * 128:(tg + 1) * 128],
                             p_=bw.ap[:, tg * 128:(tg + 1) * 128]:
                             e.tensor_copy(out=a, in_=p_),
                             wait=list(bw.rwait()) + ([('t_t', P.v('t_t'))]
                                                      if tg == 0 else []))
                pv = bts.ap[:, 0:128]
                bts.w = P.ten(lambda e, S, p_=pv,
                              s_=sfmb[:, tg * 128:(tg + 1) * 128]:
                              e.transpose(p_, s_, ident),
                              wait=bts.wwait() + [('t_v', P.v('t_v'))],
                              tag=f'sT-{sp2}-{co}-{tg}')
                bts.r = P.sca(lambda e, S, a=s_tok[tg][:, co * 128:(co + 1) * 128],
                              p_=pv: e.copy(a, p_),
                              wait=list(bts.rwait()) +
                              ([('t_v', P.v('t_v'))] if co == 0 else []))
        sdone = [('t_s', P.v('t_s')), ('t_v', P.v('t_v'))]
        for tg in range(4):
            trow = sp2 * 512 + tg * 128
            for p0_ in range(0, DM, 512):
                dg = P.dma('scalar', gt, dram['g_scr'][trow:trow + 128, p0_:p0_ + 512],
                           wait=[g_done, ('d_xg', P.v('d_xg')),
                                 ('t_v', P.v('t_v'))] + sdone, sem='d_xg')
                P.dma('scalar', xt, dram['x_span'][trow:trow + 128, p0_:p0_ + 512],
                      sem='d_xg')
                xgw = ('d_xg', P.v('d_xg'))
                P.vec(lambda e, S, s_=s_tok[tg][:, p0_:p0_ + 512]: e.tensor_tensor(
                    out=obuf, in0=s_, in1=gt, op=OP.mult),
                    wait=[xgw, ('d_out', P.v('d_out'))])
                P.vec(lambda e, S: e.tensor_tensor(
                    out=obuf, in0=obuf, in1=xt, op=OP.add))
                P.dma('gpsimd', dram['out_span'][trow:trow + 128, p0_:p0_ + 512],
                      obuf, wait=[('t_v', P.v('t_v'))], sem='d_out')
    P.emit('gpsimd', None, wait=[('d_out', P.v('d_out'))])
    A.pop(); AB.pop()

HD_SC = float(np.float32(128.0 ** -0.5))
LMAX = 6

CFG = dict(B=2, S=8192, NH=16, HD=128, FH=8192, NCORES=8)


def kernel(**inputs):
    from concourse.bass_utils import run_bass_kernel_spmd
    c, in_maps = host_prepare(CFG, inputs)
    nc, _ = build_nc(CFG)
    r = run_bass_kernel_spmd(nc, in_maps, core_ids=list(range(c['NCORES'])))
    out = np.concatenate([r.results[i]['out_span'] for i in range(c['NCORES'])],
                         axis=0)
    return out.reshape(c['B'], c['S'], c['DM'])


# version 28
# speedup vs baseline: 1.0309x; 1.0309x over previous
import math
import numpy as np
import ml_dtypes
import concourse.bass as bass
import concourse.mybir as mybir

F32 = mybir.dt.float32
BF16 = mybir.dt.bfloat16
I32 = mybir.dt.int32
AF = mybir.ActivationFunctionType
OP = mybir.AluOpType
AX = mybir.AxisListType
BF = ml_dtypes.bfloat16

C_THR = np.float32(0.501953125)


def derive(cfg):
    c = dict(cfg)
    c['DM'] = c['NH'] * 128
    c['NK'] = c['DM'] // 128
    c['SPS'] = c['NCORES'] // c['B']
    c['SPAN'] = c['S'] // c['SPS']
    c['NCH'] = c['SPAN'] // 128
    c['PSPAN'] = c['SPAN'] + 3
    c['FHC'] = c['FH'] // 128
    c['DSH'] = c['DM'] // c['NCORES']
    c['FSH'] = c['FH'] // c['NCORES']
    c['NT'] = c['SPAN'] // 128
    ts = []
    col = 0
    while col < c['PSPAN']:
        w = min(512, c['PSPAN'] - col)
        ts.append((col, w)); col += w
    c['TSUBS'] = ts
    ts2 = []
    col = 0
    while col < c['SPAN']:
        w = min(512, c['SPAN'] - col)
        ts2.append((col, w)); col += w
    c['TOKSUBS'] = ts2
    c['HSUP'] = min(4, c['FHC'])
    c['NSUP'] = c['FHC'] // c['HSUP']
    assert c['SPAN'] % 128 == 0 and c['FH'] % 128 == 0 and c['DM'] % 128 == 0
    assert c['DSH'] % 128 == 0 and c['FSH'] % 128 == 0
    return c


class Prog:
    ENGS = ("sync", "scalar", "vector", "tensor", "gpsimd")

    def __init__(self):
        self.ops = {e: [] for e in self.ENGS}
        self.val = {}
        self.semnames = []
        for s in ("d_sync", "d_scalar", "d_gpsimd", "t_t", "t_v", "t_s", "cc"):
            self.sem(s)

    def sem(self, name):
        if name not in self.val:
            self.semnames.append(name)
            self.val[name] = 0

    def v(self, s):
        return self.val[s]

    def emit(self, eng, fn=None, wait=(), inc=(), tag=''):
        for s, n in inc:
            self.val[s] += n
        self.ops[eng].append((fn, tuple(wait), tuple(inc), tag))

    def vec(self, fn, wait=(), tag=''):
        w = [('t_v', self.v('t_v'))] + [x for x in wait if x]
        self.emit('vector', fn, wait=w, inc=[('t_v', 1)], tag=tag)
        return ('t_v', self.v('t_v'))

    def sca(self, fn, wait=(), tag=''):
        w = [('t_s', self.v('t_s'))] + [x for x in wait if x]
        self.emit('scalar', fn, wait=w, inc=[('t_s', 1)], tag=tag)
        return ('t_s', self.v('t_s'))

    def ten(self, fn, wait=(), tag=''):
        self.emit('tensor', fn, wait=[x for x in wait if x], inc=[('t_t', 1)], tag=tag)
        return ('t_t', self.v('t_t'))

    def guards(self):
        return [(s, self.v(s)) for s in self.semnames if self.v(s) > 0]

    def barrier(self):
        g = self.guards()
        self.vec(lambda e, S: e.engine_nop(), wait=g)
        self.sca(lambda e, S: e.nop(), wait=g)

    def dma(self, q, out_ap, in_ap, wait=(), sem=None, tag=''):
        s = sem or ('d_' + q)
        self.sem(s)
        self.emit(q, lambda e, S: e.dma_start(out=out_ap, in_=in_ap),
                  wait=[x for x in wait if x], inc=[(s, 16)], tag=tag)
        return (s, self.v(s))

    def replay(self, block, sems, nc_scope=None):
        for name in self.ENGS:
            ops = self.ops[name]

            def body(eng, _ops=ops, _nc=getattr(block, 'nc', None)):
                for fn, waits, incs, tag in _ops:
                    for s, v_ in waits:
                        eng.wait_ge(sems[s], v_)
                    if tag and nc_scope is not None:
                        with nc_scope(tag):
                            r = fn(eng, sems) if fn is not None else None
                    else:
                        r = fn(eng, sems) if fn is not None else None
                    if incs:
                        for s, n in incs:
                            r = r.then_inc(sems[s], n)
            getattr(block, name)(body)


class Arena:

    def __init__(self, t, total_f32, name):
        self.t, self.total, self.name = t, total_f32, name
        self.off = 0
        self.marks = []
        self.peak = 0

    def alloc(self, n, tag=""):
        n = int(n)
        a = (self.off + 7) & ~7
        assert a + n <= self.total, f"{self.name} overflow: {tag} {a}+{n}>{self.total}"
        self.off = a + n
        self.peak = max(self.peak, self.off)
        return self.t[:, a:a + n]

    def push(self):
        self.marks.append(self.off)

    def pop(self):
        self.off = self.marks.pop()


class ArenaB:

    def __init__(self, base):
        self.base = base

    def alloc(self, n, tag=""):
        f = self.base.alloc((int(n) + 1) // 2, tag + "_b")
        import concourse.mybir as _mb
        return f.bitcast(_mb.dt.bfloat16)[:, 0:int(n)]

    def push(self):
        self.base.push()

    def pop(self):
        self.base.pop()


class Bank:
    def __init__(self, ap):
        self.ap = ap
        self.w = None
        self.r = None

    def wwait(self):
        return [x for x in (self.r, self.w) if x]

    def rwait(self):
        return [self.w] if self.w else []


AF32_N = 35800


def build_nc(cfg, dbg_outputs=(), phases=('p0', 'p1', 'p2', 'ag2', 'p34')):
    c = derive(cfg)
    nc = bass.Bass("TRN2", target_bir_lowering=False, debug=False)
    P = Prog()
    dram = {}

    def din(name, shape, dt=F32):
        dram[name] = nc.dram_tensor(name, list(shape), dt, kind="ExternalInput").ap()

    def dint(name, shape, dt=F32):
        kind = "ExternalOutput" if name in dbg_outputs else "Internal"
        dram[name] = nc.dram_tensor(name, list(shape), dt, kind=kind).ap()

    NH, DM, NK, SPAN, PSPAN, NCH = c['NH'], c['DM'], c['NK'], c['SPAN'], c['PSPAN'], c['NCH']
    FH, NCORES = c['FH'], c['NCORES']

    din("x_span", [SPAN, DM]); din("x_halo", [3, DM])
    din("wq", [DM, DM]); din("wk", [DM, DM]); din("wv", [DM, DM]); din("wo", [DM, DM])
    din("wb", [NH, DM])
    din("convw", [128, 12 * NK])
    din("rmsw", [128, 1])
    din("ident", [128, 128])
    din("masku", [128, 128]); din("maskud", [128, 128])
    din("sigmask", [128, NCORES])
    din("wfc_sl", [FH, c['DSH']]); din("wpr_sl", [DM, c['FSH']])
    dram['out_span'] = nc.dram_tensor("out_span", [SPAN, DM], F32,
                                      kind="ExternalOutput").ap()

    dint("w1t_sl", [c['DSH'], FH], BF16); dint("w1t", [DM, FH], BF16)
    dint("w2t_sl", [c['FSH'], DM], BF16); dint("w2t", [FH, DM], BF16)
    dint("odeg_scr", [NH * NCH * 128, 128])
    dint("qhat_scr", [NH * NCH * 128, 128])
    dint("pd_loc", [NH * 2 * 128, 128])
    dint("pd_all", [NCORES * NH * 2 * 128, 128])
    dint("g_scr", [SPAN, DM])
    dint("xnl_scr", [NK * 128, PSPAN], BF16)
    dint("wo_hi_scr", [NK * DM, 128], BF16)
    dint("wo_lo_scr", [NK * DM, 128], BF16)
    dint("wp_scr_hi", [128, NH * 3 * DM], BF16)
    dint("wp_scr_lo", [128, NH * 3 * DM], BF16)
    dint("qkv_scr", [128, NH * 3 * SPAN])
    dint("oc_hi", [128, NH * SPAN], BF16)
    dint("oc_lo", [128, NH * SPAN], BF16)
    if 'xn_dbg' in dbg_outputs:
        dint("xn_dbg", [2 * NK * 128, PSPAN], BF16)

    import contextlib
    es = contextlib.ExitStack()
    with es:
        xn_hi = es.enter_context(nc.sbuf_tensor("xn_hi", [128, NK * PSPAN], BF16))
        beta_tok = es.enter_context(nc.sbuf_tensor("beta_tok", [128, NCH * NH], F32))
        cst = es.enter_context(nc.sbuf_tensor("cst", [128, 384 + 2 + 12 * NK + NCORES], F32))
        identb = es.enter_context(nc.sbuf_tensor("identb", [128, 128], BF16))
        af32_t = es.enter_context(nc.sbuf_tensor("af32", [128, AF32_N], F32))
        psum_all = es.enter_context(nc.psum_tensor("pball", [128, 4096], F32))
        psum = [psum_all[:, i * 512:(i + 1) * 512] for i in range(8)]

        T = dict(
            xn_hi=xn_hi, beta_tok=beta_tok,
            ident=cst[:, 0:128], masku=cst[:, 128:256], maskud=cst[:, 256:384],
            rmsw=cst[:, 384:385], onesc=cst[:, 385:386],
            convw=cst[:, 386:386 + 12 * NK],
            sigmask=cst[:, 386 + 12 * NK:386 + 12 * NK + NCORES],
            identb=identb[:, :],
            af32_t=af32_t,
            psum=psum, psum_all=psum_all[:, :], banks=[Bank(p) for p in psum])
        _ar = Arena(af32_t, AF32_N, 'ar')
        T['A'] = _ar
        T['AB'] = ArenaB(_ar)

        emit_all(P, c, dram, T, phases, dbg_outputs)

        sems = {}
        for s in P.semnames:
            sems[s] = es.enter_context(nc.semaphore(s))
        with nc.Block() as block:
            P.replay(block, sems, nc_scope=nc.named_scope)
    return nc, c


def emit_all(P, c, dram, T, phases=('p0', 'p1', 'p2', 'ag2', 'p34'), dbg_outputs=()):
    for name, ap in (("ident", T['ident']), ("masku", T['masku']),
                     ("maskud", T['maskud']), ("rmsw", T['rmsw']),
                     ("convw", T['convw']), ("sigmask", T['sigmask'])):
        P.dma('sync', ap, dram[name][:, :])
    cl = ('d_sync', P.v('d_sync'))
    P.vec(lambda e, S: e.tensor_copy(out=T['identb'], in_=T['ident']), wait=[cl])
    P.vec(lambda e, S: e.memset(T['onesc'], 1.0))

    if 'p0' in phases:
        emit_p0(P, c, dram, T)
    if 'p1' in phases:
        P.barrier()
        emit_p1(P, c, dram, T)
    if 'p2' in phases:
        P.barrier()
        emit_p2(P, c, dram, T)
    if 'ag2' in phases:
        P.sem('d_pd')
        P.emit('gpsimd', lambda e, S: e.collective_compute(
            "AllGather", OP.bypass, ins=[dram['pd_loc'][:, :].opt()],
            outs=[dram['pd_all'][:, :].opt()],
            replica_groups=[list(range(c['NCORES']))]),
            wait=[('d_pd', P.v('d_pd'))], inc=[('cc', 1)])
    if 'p34' in phases:
        P.barrier()
        emit_p34(P, c, dram, T)
    if 'xn_dbg' in dbg_outputs:
        NK, PSPAN = c['NK'], c['PSPAN']
        for k in range(NK):
            P.dma('sync', dram['xn_dbg'][k * 128:(k + 1) * 128, :],
                  T['xn_hi'][:, k * PSPAN:(k + 1) * PSPAN],
                  wait=[('t_v', P.v('t_v')), ('t_s', P.v('t_s'))])
            P.dma('sync', dram['xn_dbg'][(NK + k) * 128:(NK + k + 1) * 128, :],
                  T['xn_lo'][:, k * PSPAN:(k + 1) * PSPAN])
    if 'p4' not in phases and 'p34' not in phases:
        z = T['A'].alloc(c['DM'], 'zout')
        P.vec(lambda e, S: e.memset(z, 0.0))
        for t in range(c['NT']):
            P.dma('sync', dram['out_span'][t * 128:(t + 1) * 128, :], z,
                  wait=[('t_v', P.v('t_v'))])
    P.emit('sync', None, wait=[('d_sync', P.v('d_sync')),
                               ('d_scalar', P.v('d_scalar')),
                               ('d_gpsimd', P.v('d_gpsimd'))])



def host_prepare(cfg, ins):
    c = derive(cfg)
    B, S, DM, NH = c['B'], c['S'], c['DM'], c['NH']
    NCORES, SPAN, NK = c['NCORES'], c['SPAN'], c['NK']
    x = np.ascontiguousarray(ins['L0_attn_x'].reshape(B * S, DM))
    convw = np.zeros((128, 12 * NK), np.float32)
    for ti, name in enumerate(('conv_q', 'conv_k', 'conv_v')):
        cw = ins['L0_attn_' + name]
        for tap in range(4):
            for k in range(NK):
                convw[:, (ti * 4 + tap) * NK + k] = cw[k * 128:(k + 1) * 128, tap]
    ident = np.eye(128, dtype=np.float32)
    masku = (np.arange(128)[:, None] < np.arange(128)[None, :]).astype(np.float32)
    maskud = (np.arange(128)[:, None] <= np.arange(128)[None, :]).astype(np.float32)
    rmsw = np.ascontiguousarray(ins['L0_attn_rms_w'].reshape(128, 1))
    in_maps = []
    for core in range(NCORES):
        t0 = core * SPAN
        sstart = t0 % S
        halo = x[t0 - 3:t0] if sstart > 0 else np.zeros((3, DM), np.float32)
        first = (core // c['SPS']) * c['SPS']
        sm = np.zeros((128, NCORES), np.float32)
        for c2 in range(NCORES):
            if first <= c2 < core:
                sm[:, c2] = 1.0
        m = {
            'x_span': x[t0:t0 + SPAN],
            'x_halo': np.ascontiguousarray(halo),
            'wq': ins['L0_attn_Wq'], 'wk': ins['L0_attn_Wk'],
            'wv': ins['L0_attn_Wv'], 'wo': ins['L0_attn_Wo'],
            'wb': ins['L0_attn_Wb'], 'convw': convw, 'rmsw': rmsw,
            'ident': ident, 'masku': masku, 'maskud': maskud,
            'sigmask': sm,
            'wfc_sl': np.ascontiguousarray(
                ins['L0_ffn_W_fc'][:, core * c['DSH']:(core + 1) * c['DSH']]),
            'wpr_sl': np.ascontiguousarray(
                ins['L0_ffn_W_proj'][:, core * c['FSH']:(core + 1) * c['FSH']]),
        }
        in_maps.append(m)
    return c, in_maps






def emit_p1(P, c, dram, T):
    A = T['A']
    NK, PSPAN, NT, DM = c['NK'], c['PSPAN'], c['NT'], c['DM']
    banks = T['banks']
    P.sem('d_xl')
    A.push()
    xt = A.alloc(DM, 'xt')
    xsq = A.alloc(DM, 'xsq')
    st = A.alloc(8, 'st')
    xn_tok = A.alloc(DM, 'xn_tok')
    AB = T['AB']
    lost = AB.alloc(NK * 128, 'lost')

    def ln_tile(src_ap, nrows, dst_col, prev_dma):
        dw = P.dma('sync', xt[0:nrows, :], src_ap, wait=prev_dma)
        P.sca(lambda e, S, n=nrows: e.activation(
            out=xsq[0:n, :], in_=xt[0:n, :], func=AF.Square), wait=[dw])
        P.vec(lambda e, S, n=nrows: e.tensor_reduce(
            out=st[0:n, 0:1], in_=xt[0:n, :], op=OP.add, axis=AX.X), wait=[dw])
        P.vec(lambda e, S, n=nrows: e.tensor_reduce(
            out=st[0:n, 1:2], in_=xsq[0:n, :], op=OP.add, axis=AX.X),
            wait=[('t_s', P.v('t_s'))])
        P.vec(lambda e, S, n=nrows: e.tensor_scalar(
            out=st[0:n, 2:4], in0=st[0:n, 0:2], scalar1=float(np.float32(1.0 / c['DM'])),
            scalar2=None, op0=OP.mult))
        P.vec(lambda e, S, n=nrows: e.tensor_tensor(
            out=st[0:n, 4:5], in0=st[0:n, 2:3], in1=st[0:n, 2:3], op=OP.mult))
        P.vec(lambda e, S, n=nrows: e.scalar_tensor_tensor(
            out=st[0:n, 4:5], in0=st[0:n, 3:4], scalar=float(np.float32(1.0)),
            in1=st[0:n, 4:5], op0=OP.mult, op1=OP.subtract))
        P.vec(lambda e, S, n=nrows: e.tensor_scalar(
            out=st[0:n, 4:5], in0=st[0:n, 4:5], scalar1=float(np.float32(1e-5)),
            scalar2=None, op0=OP.add))
        P.sca(lambda e, S, n=nrows: e.activation(
            out=st[0:n, 5:6], in_=st[0:n, 4:5], func=AF.Sqrt),
            wait=[('t_v', P.v('t_v'))])
        P.vec(lambda e, S, n=nrows: e.reciprocal(
            out=st[0:n, 6:7], in_=st[0:n, 5:6]), wait=[('t_s', P.v('t_s'))])
        P.vec(lambda e, S, n=nrows: e.scalar_tensor_tensor(
            out=st[0:n, 7:8], in0=st[0:n, 2:3], scalar=float(np.float32(-1.0)),
            in1=st[0:n, 6:7], op0=OP.mult, op1=OP.mult))
        P.sca(lambda e, S, n=nrows: e.activation(
            out=xn_tok[0:n, :], in_=xt[0:n, :], func=AF.Identity,
            scale=st[0:n, 6:7], bias=st[0:n, 7:8]), wait=[('t_v', P.v('t_v'))])
        sv = ('t_s', P.v('t_s'))
        for k in range(NK):
            bank = banks[4 + (k % 4)]
            pv = bank.ap[:, 0:nrows]
            tv = P.ten(lambda e, S, p_=pv, s_=xn_tok[0:nrows, k * 128:(k + 1) * 128],
                       i_=T['ident'][0:nrows, 0:nrows]: e.transpose(p_, s_, i_),
                       wait=[sv] + bank.wwait())
            bank.w = tv
            hi_ap = T['xn_hi'][:, k * PSPAN + dst_col: k * PSPAN + dst_col + nrows]
            lo_ap = lost[:, k * 128: k * 128 + nrows]
            P.sca(lambda e, S, a=hi_ap, p_=pv: e.copy(a, p_), wait=[tv])
            rv = P.vec(lambda e, S, a=lo_ap, p_=pv, h=hi_ap: e.tensor_tensor(
                out=a, in0=p_, in1=h, op=OP.subtract),
                wait=[tv, ('t_s', P.v('t_s')), ('d_xl', P.v('d_xl'))])
            bank.r = rv
        P.dma('gpsimd',
              dram['xnl_scr'].rearrange("(k p) col -> p k col", p=128)[:, :, dst_col:dst_col + nrows],
              lost.rearrange("p (k col) -> p k col", col=128)[:, :, 0:nrows],
              wait=[('t_v', P.v('t_v'))], sem='d_xl')
        return [('t_v', P.v('t_v')), ('t_s', P.v('t_s'))]

    prev = ln_tile(dram['x_halo'][:, :], 3, 0, P.guards())
    for t in range(NT):
        prev = ln_tile(dram['x_span'][t * 128:(t + 1) * 128, :], 128,
                       3 + t * 128, prev)
    A.pop()




def emit_p0(P, c, dram, T):
    A, AB, banks = T['A'], T['AB'], T['banks']
    for which, Wsl, nrows, ncols, wTsl in (
            ('fc', dram['wfc_sl'], c['FH'], c['DSH'], dram['w1t_sl']),
            ('pr', dram['wpr_sl'], c['DM'], c['FSH'], dram['w2t_sl'])):
        A.push(); AB.push()
        NTI = nrows // 128
        G = ncols // 64
        NOC = ncols // 128
        P.sem('d_ccin')
        wt_f = A.alloc(ncols, 'wt_f')
        wabs = A.alloc(ncols, 'wabs')
        wab = AB.alloc(ncols, 'wab')
        sg = AB.alloc(ncols, 'sg')
        u = AB.alloc(ncols, 'u')
        m1 = AB.alloc(ncols, 'm1')
        w1tile = AB.alloc(ncols, 'w1tile')
        gs = A.alloc(G, 'gs')
        scb = AB.alloc(G, 'scb')
        tt = A.alloc(G, 'tt')
        tst = (AB.alloc(128, 'tst_a'), AB.alloc(128, 'tst_b'))
        for ti in range(NTI):
            dw = P.dma('sync', wt_f, Wsl[ti * 128:(ti + 1) * 128, :],
                       wait=[('t_v', P.v('t_v')), ('t_s', P.v('t_s'))])
            P.vec(lambda e, S, a=wabs.bitcast(I32), b=wt_f.bitcast(I32):
                e.tensor_scalar(out=a, in0=b, scalar1=0x7FFFFFFF, scalar2=None,
                op0=OP.bitwise_and), wait=[dw, ('d_ccin', P.v('d_ccin'))])
            P.vec(lambda e, S, a=wab, b=wabs: e.tensor_copy(out=a, in_=b))
            P.vec(lambda e, S, a=gs.rearrange("p (g o) -> p g o", o=1),
                b=wab.rearrange("p (g o) -> p g o", o=64):
                e.tensor_reduce(out=a, in_=b, op=OP.add, axis=AX.X))
            P.vec(lambda e, S, a=scb, b=gs: e.tensor_scalar(
                out=a, in0=b, scalar1=float(np.float32(1.0 / 64)), scalar2=None,
                op0=OP.mult))
            P.vec(lambda e, S, a=scb: e.tensor_scalar(
                out=a, in0=a, scalar1=float(np.float32(BF(1e-8))), scalar2=None,
                op0=OP.max))
            P.vec(lambda e, S, a=tt, b=scb: e.tensor_scalar(
                out=a, in0=b, scalar1=float(C_THR), scalar2=None, op0=OP.mult))
            P.vec(lambda e, S, a=u.rearrange("p (g o) -> p g o", o=64),
                b=wab.rearrange("p (g o) -> p g o", o=64),
                t_=tt.rearrange("p (g o) -> p g o", o=1).broadcast_to([128, G, 64]):
                e.tensor_tensor(out=a, in0=b, in1=t_, op=OP.is_gt))
            P.sca(lambda e, S, a=sg, b=wt_f: e.activation(out=a, in_=b, func=AF.Sign),
                  wait=[dw, ('d_ccin', P.v('d_ccin'))])
            P.vec(lambda e, S, a=m1.rearrange("p (g o) -> p g o", o=64),
                b=u.rearrange("p (g o) -> p g o", o=64),
                t_=scb.rearrange("p (g o) -> p g o", o=1).broadcast_to([128, G, 64]):
                e.tensor_tensor(out=a, in0=b, in1=t_, op=OP.mult))
            P.vec(lambda e, S, a=w1tile, b=m1, d=sg: e.tensor_tensor(
                out=a, in0=b, in1=d, op=OP.mult),
                wait=[('t_s', P.v('t_s'))])
            vv = ('t_v', P.v('t_v'))
            for oc in range(NOC):
                bank = banks[oc % 2]
                pv = bank.ap[:, 0:64].bitcast(BF16)
                tv = P.ten(lambda e, S, p_=pv,
                           s_=w1tile[:, oc * 128:(oc + 1) * 128]:
                           e.transpose(p_, s_, T['identb']),
                           wait=[vv] + bank.wwait())
                bank.w = tv
                sb = tst[oc % 2]
                bank.r = P.vec(lambda e, S, d=sb, p_=pv:
                               e.tensor_copy(out=d, in_=p_),
                               wait=[tv, ('d_ccin', P.v('d_ccin'))])
                P.dma('sync', wTsl[oc * 128:(oc + 1) * 128,
                                   ti * 128:(ti + 1) * 128], sb,
                      wait=[('t_v', P.v('t_v'))], sem='d_ccin')
        A.pop(); AB.pop()
    w = ('d_ccin', P.v('d_ccin'))
    P.emit('gpsimd', lambda e, S: e.collective_compute(
        "AllGather", OP.bypass, ins=[dram['w1t_sl'][:, :].opt()],
        outs=[dram['w1t'][:, :].opt()],
        replica_groups=[list(range(c['NCORES']))]), wait=[w], inc=[('cc', 1)])
    P.emit('gpsimd', lambda e, S: e.collective_compute(
        "AllGather", OP.bypass, ins=[dram['w2t_sl'][:, :].opt()],
        outs=[dram['w2t'][:, :].opt()],
        replica_groups=[list(range(c['NCORES']))]), wait=[w], inc=[('cc', 1)])



NI = 3


def emit_p2(P, c, dram, T):
    NH, NK, SPAN, PSPAN, NCH = c['NH'], c['NK'], c['SPAN'], c['PSPAN'], c['NCH']
    DM = c['DM']
    A, AB = T['A'], T['AB']
    banks = T['banks']
    xn_hi = T['xn_hi']
    ident, masku, maskud = T['ident'], T['masku'], T['maskud']
    beta_tok = T['beta_tok']
    NTOK = len(c['TOKSUBS'])
    for s in ('d_oq', 'd_pd', 'd_xlo0', 'd_xlo1', 'd_wst', 'd_wsl', 'd_qkv'):
        P.sem(s)
    for par in range(2):
        P.sem('d_t3_%d' % par)

    def tmm(bank, lhsT, rhs, start, stop, wait=(), width=512, rows=128, tag=''):
        ws = [x for x in wait if x]
        if start:
            ws += bank.wwait()
        return P.ten(lambda e, S, a_=bank.ap[0:rows, 0:width], l_=lhsT, r_=rhs,
                     st=start, sp=stop: e.matmul(a_, l_, r_, start=st, stop=sp),
                     wait=ws, tag=tag)

    def mark_w(bank):
        bank.w = ('t_t', P.v('t_t'))

    def evacv(bank, fn, wait=()):
        bank.r = P.vec(fn, wait=list(bank.rwait()) + list(wait))
        return bank.r

    def evacs(bank, fn, wait=()):
        bank.r = P.sca(fn, wait=list(bank.rwait()) + list(wait))
        return bank.r

    def transpose_to(bank, src, n=128, wait=()):
        ws = bank.wwait() + [x for x in wait if x]
        pv = bank.ap[:, 0:n]
        bank.w = P.ten(lambda e, S, p_=pv, s_=src, i_=ident[0:n, 0:n]:
                       e.transpose(p_, s_, i_), wait=ws)
        return pv

    A.push(); AB.push()
    wnat = A.alloc(DM, 'wnat')
    pre = A.alloc(PSPAN, 'pre')
    acc_a = A.alloc(512, 'acc_a'); acc_b = A.alloc(512, 'acc_b')
    silb = (A.alloc(512, 'silb_a'), A.alloc(512, 'silb_b'))
    xlo2 = (AB.alloc(NK * 512, 'xlo_a'), AB.alloc(NK * 512, 'xlo_b'))
    wph2 = (AB.alloc(NK * 128, 'wph_a'), AB.alloc(NK * 128, 'wph_b'))
    wpl2 = (AB.alloc(NK * 128, 'wpl_a'), AB.alloc(NK * 128, 'wpl_b'))
    woh = AB.alloc(128, 'woh'); wol = AB.alloc(128, 'wol')
    bneg_tok = A.alloc(NCH * NH, 'bneg_tok')
    sq_shared = A.alloc(128, 'sq_shared')

    def xh(k, col0, wdt):
        return xn_hi[:, k * PSPAN + col0: k * PSPAN + col0 + wdt]

    def xlo_dma(tc0, tw, buf):
        return P.dma('scalar',
                     xlo2[buf].rearrange("p (k f) -> p k f", f=512)[:, :, 0:tw],
                     dram['xnl_scr'].rearrange("(k p) col -> p k col", p=128)
                     [:, :, tc0:tc0 + tw],
                     wait=[('t_t', P.v('t_t')), ('d_xl', P.v('d_xl'))],
                     sem='d_xlo%d' % buf)

    def xl(k, tw, buf):
        return xlo2[buf][:, k * 512: k * 512 + tw]

    Wsrc = (dram['wq'], dram['wk'], dram['wv'])

    A.push()
    wnat2 = (A.alloc(DM, 'wn2a'), A.alloc(DM, 'wn2b'))
    for h in range(NH):
        for ti in range(3):
            i = h * 3 + ti
            dw = P.dma('sync', wnat2[i % 2], Wsrc[ti][h * 128:(h + 1) * 128, :],
                       wait=P.guards() if i == 0 else [('t_t', P.v('t_t'))])
            wstg = ('d_wst', P.v('d_wst'))
            for k in range(NK):
                bkw = banks[1 + (k % 2)]
                pv = transpose_to(bkw, wnat2[i % 2][:, k * 128:(k + 1) * 128],
                                  wait=[dw])
                hi_ap = wph2[0][:, k * 128:(k + 1) * 128]
                lo_ap = wpl2[0][:, k * 128:(k + 1) * 128]
                evacs(bkw, lambda e, S, a=hi_ap, p_=pv: e.copy(a, p_),
                      wait=[wstg])
                evacv(bkw, lambda e, S, a=lo_ap, p_=pv, h_=hi_ap:
                      e.tensor_tensor(out=a, in0=p_, in1=h_, op=OP.subtract),
                      wait=[('t_s', P.v('t_s')), wstg])
            P.dma('scalar', dram['wp_scr_hi'][:, i * DM:(i + 1) * DM], wph2[0],
                  wait=[('t_s', P.v('t_s')), ('t_v', P.v('t_v'))], sem='d_wst')
            P.dma('scalar', dram['wp_scr_lo'][:, i * DM:(i + 1) * DM], wpl2[0],
                  wait=[('t_v', P.v('t_v'))], sem='d_wst')
    A.pop()
    wst_total = ('d_wst', P.v('d_wst'))

    A.push()
    beta_fm = A.alloc(SPAN, 'beta_fm')
    wbt_hi = AB.alloc(NK * 16, 'wbt_hi'); wbt_lo = AB.alloc(NK * 16, 'wbt_lo')
    dw = P.dma('sync', wnat[0:NH, :], dram['wb'][:, :],
               wait=[('t_t', P.v('t_t')), ('t_v', P.v('t_v')),
                     ('t_s', P.v('t_s'))])
    for k in range(NK):
        bk_t = banks[1 + (k % 2)]
        pv = transpose_to(bk_t, wnat[0:NH, k * 128:(k + 1) * 128], n=NH, wait=[dw])
        hi_ap = wbt_hi[:, k * 16:k * 16 + NH]
        lo_ap = wbt_lo[:, k * 16:k * 16 + NH]
        evacs(bk_t, lambda e, S, a=hi_ap, p_=pv: e.copy(a, p_))
        evacv(bk_t, lambda e, S, a=lo_ap, p_=pv, h=hi_ap: e.tensor_tensor(
            out=a, in0=p_, in1=h, op=OP.subtract), wait=[('t_s', P.v('t_s'))])
    wb_done = [('t_v', P.v('t_v')), ('t_s', P.v('t_s'))]
    bk_b = banks[0]
    dxls = {0: xlo_dma(3 + c['TOKSUBS'][0][0], c['TOKSUBS'][0][1], 0)}
    for si, (tc0, tw) in enumerate(c['TOKSUBS']):
        bf = si % 2
        if si + 1 < len(c['TOKSUBS']):
            dxls[si + 1] = xlo_dma(3 + c['TOKSUBS'][si + 1][0],
                                   c['TOKSUBS'][si + 1][1], (si + 1) % 2)
        dxl = dxls[si]
        nmm = NK * 3
        i = 0
        for k in range(NK):
            for wap, xap in ((wbt_hi[:, k * 16:k * 16 + NH], xh(k, 3 + tc0, tw)),
                             (wbt_hi[:, k * 16:k * 16 + NH], xl(k, tw, bf)),
                             (wbt_lo[:, k * 16:k * 16 + NH], xh(k, 3 + tc0, tw))):
                tmm(bk_b, wap, xap, start=(i == 0), stop=(i == nmm - 1),
                    wait=(wb_done + [dxl]) if i == 0 else (), width=tw, rows=NH)
                i += 1
        mark_w(bk_b)
        evacs(bk_b, lambda e, S, a=beta_fm[0:NH, tc0:tc0 + tw],
              p_=bk_b.ap[0:NH, 0:tw]: e.activation(out=a, in_=p_, func=AF.Sigmoid))
    for j in range(NCH):
        pv = transpose_to(banks[1], beta_fm[0:NH, j * 128:(j + 1) * 128], n=NH,
                          wait=[('t_s', P.v('t_s'))])
        evacv(banks[1], lambda e, S, a=beta_tok[:, j * NH:(j + 1) * NH],
              p_=banks[1].ap[:, 0:NH]: e.tensor_copy(out=a, in_=p_))
    A.pop()
    P.vec(lambda e, S, a=bneg_tok[:, 0:NCH * NH], b=beta_tok[:, 0:NCH * NH]:
          e.tensor_scalar(out=a, in0=b, scalar1=-1.0, scalar2=None, op0=OP.mult))

    MH = 3
    tb3 = {}
    for nm in ('q_rt', 'k_rt', 'qn', 'q_fm', 'k_fm', 'gm', 'qpre', 'm1', 'sq',
               'bTV', 'y3', 'ost', 'qst'):
        tb3[nm] = A.alloc(MH * 128, nm + '3')
    tb3['nst'] = A.alloc(16, 'nst3')
    tb3['fpow'] = [A.alloc(MH * 128, f'fp3_{i}') for i in range(LMAX + 1)]
    tb3['ftc'] = (A.alloc(MH * 128, 'ftc3a'), A.alloc(MH * 128, 'ftc3b'))
    tb3['mb'] = (A.alloc(MH * 128, 'mb3a'), A.alloc(MH * 128, 'mb3b'))
    tb3['u_a'] = A.alloc(MH * 256, 'ua3')
    tb3['D'] = (A.alloc(MH * 128, 'D3a'), A.alloc(MH * 128, 'D3b'))
    tb3['P'] = (A.alloc(MH * 128, 'P3a'), A.alloc(MH * 128, 'P3b'))
    tb3['tile'] = [[A.alloc(MH * 128, f'tl3_{par}_{ci}') for ci in range(3)]
                   for par in range(2)]
    PSA = T['psum_all']

    class TrBank:
        def __init__(self, base):
            self.base = base
            self.w = ('t_t', P.v('t_t'))
            self.r = [('t_v', P.v('t_v')), ('t_s', P.v('t_s'))]

        def mmout(self, i, w):
            b = self.base + i
            return PSA[:, b * 512:b * 512 + w]

        def view(self, nh, w, off=0):
            return (PSA[:, self.base * 512:(self.base + nh) * 512]
                    .rearrange("p (b c) -> p b c", c=512)[:, :, off:off + w])

        def wwait(self):
            out = list(self.r) if isinstance(self.r, list) else ([self.r] if self.r else [])
            if self.w:
                out.append(self.w)
            return out

        def rwait(self):
            return [self.w] if self.w else []

    tA = TrBank(0)
    tB = TrBank(3)
    bP = (banks[6], banks[7])

    def proj_gen(h):
        seq = [(ti, s) for ti in range(3) for s in range(len(c['TSUBS']))]
        dxls = {}
        def pf(idx):
            if idx < len(seq) and idx not in dxls:
                ti_, s_ = seq[idx]
                tc0_, tw_ = c['TSUBS'][s_]
                dxls[idx] = xlo_dma(tc0_, tw_, idx % 2)
        pf(0)
        for ti in range(3):
            i = h * 3 + ti
            pb = i % 2
            whi, wlo = wph2[pb], wpl2[pb]
            P.dma('scalar', whi, dram['wp_scr_hi'][:, i * DM:(i + 1) * DM],
                  wait=[wst_total, ('t_t', P.v('t_t'))], sem='d_wsl')
            P.dma('scalar', wlo, dram['wp_scr_lo'][:, i * DM:(i + 1) * DM],
                  sem='d_wsl')
            slab_w = ('d_wsl', P.v('d_wsl'))
            yield
            for s_i, (tc0, tw) in enumerate(c['TSUBS']):
                idx = ti * len(c['TSUBS']) + s_i
                dxl = dxls[idx]
                pf(idx + 1)
                bkp = bP[s_i % 2]
                nmm = NK * 3
                i2 = 0
                for k in range(NK):
                    for wap, xap in ((whi[:, k * 128:(k + 1) * 128], xh(k, tc0, tw)),
                                     (whi[:, k * 128:(k + 1) * 128], xl(k, tw, idx % 2)),
                                     (wlo[:, k * 128:(k + 1) * 128], xh(k, tc0, tw))):
                        tmm(bkp, wap, xap, start=(i2 == 0), stop=(i2 == nmm - 1),
                            wait=[slab_w, dxl] if i2 == 0 else (), width=tw,
                            tag=f'pj-{h}-{ti}-{s_i}')
                        i2 += 1
                    if k % 2 == 1:
                        yield
                mark_w(bkp)
                evacv(bkp, lambda e, S, a=pre[:, tc0:tc0 + tw],
                      p_=bkp.ap[:, 0:tw]: e.tensor_copy(out=a, in_=p_))
                yield
            for s_i, (tc0, tw) in enumerate(c['TOKSUBS']):
                def cw(tap, _ti=ti):
                    i_ = (_ti * 4 + tap) * NK + h
                    return T['convw'][:, i_:i_ + 1]
                P.vec(lambda e, S, o=acc_a[:, 0:tw], i0=pre[:, tc0:tc0 + tw],
                      s_=cw(0): e.tensor_scalar(out=o, in0=i0, scalar1=s_,
                      scalar2=None, op0=OP.mult), wait=[('t_s', P.v('t_s'))])
                order = ((acc_a, acc_b), (acc_b, acc_a), (acc_a, acc_b))
                for tap in (1, 2, 3):
                    si_, di = order[tap - 1]
                    P.vec(lambda e, S, o=di[:, 0:tw],
                          i0=pre[:, tc0 + tap:tc0 + tap + tw], s_=cw(tap),
                          i1=si_[:, 0:tw]: e.scalar_tensor_tensor(
                              out=o, in0=i0, scalar=s_, in1=i1,
                              op0=OP.mult, op1=OP.add))
                sb = silb[s_i % 2]
                P.sca(lambda e, S, o=sb[:, 0:tw], i0=acc_b[:, 0:tw]:
                      e.activation(out=o, in_=i0, func=AF.Silu),
                      wait=[('t_v', P.v('t_v')), ('d_qkv', P.v('d_qkv'))])
                P.dma('gpsimd', dram['qkv_scr'][:, i * SPAN + tc0:i * SPAN + tc0 + tw],
                      sb[:, 0:tw], wait=[('t_s', P.v('t_s'))], sem='d_qkv')
                yield

    def bc1(ap_col, nh):
        return ap_col.rearrange("p (b o) -> p b o", o=1).broadcast_to([128, nh, 128])

    def bcm(m_ap, nh):
        return m_ap.rearrange("p (o c) -> p o c", o=1).broadcast_to([128, nh, 128])

    def r3(flat, nh, w=128):
        return flat.rearrange("p (b c) -> p b c", c=w)[:, 0:nh, :]

    def triple_gen(h0, nh):
        fpow, ftc = tb3['fpow'], tb3['ftc']
        pd_guard = ('d_pd', P.v('d_pd'))

        def mm3(tb, lhs_f, rhs_f, wait, width=128, tag='', pair_rhs=None):
            ws = list(wait) + tb.wwait()
            for i in range(nh):
                if pair_rhs is not None:
                    P.ten(lambda e, S, a_=tb.mmout(i, width),
                          l_=lhs_f(i), r_=rhs_f(i):
                          e.matmul(a_, l_, r_, start=True, stop=False),
                          wait=ws if i == 0 else (), tag=tag)
                    P.ten(lambda e, S, a_=tb.mmout(i, width),
                          l_=pair_rhs[0](i), r_=pair_rhs[1](i):
                          e.matmul(a_, l_, r_, start=False, stop=True))
                else:
                    P.ten(lambda e, S, a_=tb.mmout(i, width),
                          l_=lhs_f(i), r_=rhs_f(i):
                          e.matmul(a_, l_, r_, start=True, stop=True),
                          wait=ws if i == 0 else (), tag=tag)
            tb.w = ('t_t', P.v('t_t'))

        def tr3(tb, src_f, wait, tag=''):
            ws = list(wait) + tb.wwait()
            for i in range(nh):
                P.ten(lambda e, S, p_=tb.mmout(i, 128), s_=src_f(i),
                      i_=ident: e.transpose(p_, s_, i_),
                      wait=ws if i == 0 else (), tag=tag)
            tb.w = ('t_t', P.v('t_t'))

        def ev_v(tb, fn, wait=()):
            tb.r = P.vec(fn, wait=tb.rwait() + list(wait))
            return tb.r

        def ev_s(tb, fn, wait=()):
            tb.r = P.sca(fn, wait=tb.rwait() + list(wait))
            return tb.r

        def col(buf, i, w=128):
            return buf[:, i * w:(i + 1) * w]

        def tsidx(jj):
            tpos = jj * 128
            s = 0
            for si_, (tc0_, tw_) in enumerate(c['TOKSUBS']):
                if tc0_ <= tpos:
                    s = si_
            return s

        tile_w = {}

        def emit_tiles(jj):
            gate = ('d_qkv', 16 * ((3 * (h0 + nh - 1) + 2) * NTOK + tsidx(jj) + 1))
            for ci in range(3):
                P.dma('sync',
                      r3(tb3['tile'][jj % 2][ci], nh),
                      dram['qkv_scr'].rearrange("p (hh c) -> p hh c", c=3 * SPAN)
                      [:, h0:h0 + nh, ci * SPAN + jj * 128:ci * SPAN + (jj + 1) * 128],
                      wait=[gate, ('t_t', P.v('t_t'))] if ci == 0 else [gate],
                      sem='d_t3_%d' % (jj % 2))
            tile_w[jj] = ('d_t3_%d' % (jj % 2), P.v('d_t3_%d' % (jj % 2)))

        emit_tiles(0)
        for j in range(NCH):
            bslc = beta_tok[:, j * NH + h0: j * NH + h0 + nh]
            bnslc = bneg_tok[:, j * NH + h0: j * NH + h0 + nh]
            Dprev = tb3['D'][(j + 1) % 2]; Pprev = tb3['P'][(j + 1) % 2]
            Dnew = tb3['D'][j % 2]; Pnew = tb3['P'][j % 2]
            tg = ('t_t', P.v('t_t'))
            if j + 1 < NCH:
                emit_tiles(j + 1)
            tw_j = tile_w.pop(j)
            tl = tb3['tile'][j % 2]
            for ci, dst3 in ((0, r3(tb3['q_rt'], nh)),
                             (1, r3(tb3['k_rt'], nh)),
                             (2, r3(tb3['u_a'], nh, 256)[:, :, 0:128])):
                tr3(tA, lambda i, _t=tl[ci]: col(_t, i), wait=[tw_j], tag=f'tT-{h0}-{j}')
                ev_s(tA, lambda e, S, a=dst3, p_=tA.view(nh, 128):
                     e.copy(a, p_), wait=[tg])
                yield
            for (raw, ntok3, fm, is_q) in (
                    (tb3['q_rt'], r3(tb3['qn'], nh), tb3['q_fm'], True),
                    (tb3['k_rt'], r3(tb3['u_a'], nh, 256)[:, :, 128:256],
                     tb3['k_fm'], False)):
                P.vec(lambda e, S, a=tb3['sq'][:, 0:nh * 128],
                      b=raw[:, 0:nh * 128]: e.tensor_tensor(
                          out=a, in0=b, in1=b, op=OP.mult),
                      wait=[('t_s', P.v('t_s'))])
                P.vec(lambda e, S, a=tb3['nst'][:, 0:nh]
                      .rearrange("p (b o) -> p b o", o=1),
                      b=r3(tb3['sq'], nh): e.tensor_reduce(
                          out=a, in_=b, op=OP.add, axis=AX.X))
                P.vec(lambda e, S, a=tb3['nst'][:, 4:4 + nh],
                      b=tb3['nst'][:, 0:nh]: e.tensor_scalar(
                          out=a, in0=b, scalar1=1e-6, scalar2=None, op0=OP.add))
                yield
                P.sca(lambda e, S, a=tb3['nst'][:, 8:8 + nh],
                      b=tb3['nst'][:, 4:4 + nh]: e.activation(
                          out=a, in_=b, func=AF.Sqrt),
                      wait=[('t_v', P.v('t_v'))])
                P.vec(lambda e, S, a=tb3['nst'][:, 12:12 + nh],
                      b=tb3['nst'][:, 8:8 + nh]: e.reciprocal(out=a, in_=b),
                      wait=[('t_s', P.v('t_s'))])
                if is_q:
                    P.vec(lambda e, S, a=tb3['nst'][:, 12:12 + nh]:
                          e.tensor_scalar(out=a, in0=a, scalar1=HD_SC,
                                          scalar2=None, op0=OP.mult))
                P.vec(lambda e, S, a=ntok3, b=r3(raw, nh),
                      s_=bc1(tb3['nst'][:, 12:12 + nh], nh): e.tensor_tensor(
                          out=a, in0=b, in1=s_, op=OP.mult), wait=[tg])
                yield
                src3 = tb3['qn'] if is_q else None
                if is_q:
                    tr3(tA, lambda i: col(tb3['qn'], i),
                        wait=[('t_v', P.v('t_v'))], tag=f'nT-{h0}-{j}')
                else:
                    tr3(tA, lambda i: tb3['u_a'][:, i * 256 + 128:(i + 1) * 256],
                        wait=[('t_v', P.v('t_v'))], tag=f'nT-{h0}-{j}')
                ev_s(tA, lambda e, S, a=r3(fm, nh), p_=tA.view(nh, 128):
                     e.copy(a, p_))
                yield
            norm_done = [('t_s', P.v('t_s'))]
            mm3(tB, lambda i: col(tb3['k_fm'], i), lambda i: col(tb3['k_fm'], i),
                wait=norm_done, tag=f'KK-{h0}-{j}')
            ev_v(tB, lambda e, S, a=r3(fpow[0], nh), p_=tB.view(nh, 128),
                 b_=bc1(bnslc, nh): e.tensor_tensor(
                     out=a, in0=p_, in1=b_, op=OP.mult))
            P.vec(lambda e, S, a=r3(fpow[0], nh), m_=bcm(masku, nh):
                  e.tensor_tensor(out=a, in0=a, in1=m_, op=OP.mult))
            yield
            mm3(tB, lambda i: col(tb3['k_fm'], i), lambda i: col(tb3['q_fm'], i),
                wait=[('t_s', P.v('t_s'))], tag=f'G-{h0}-{j}')
            ev_v(tB, lambda e, S, a=r3(tb3['gm'], nh), p_=tB.view(nh, 128),
                 m_=bcm(maskud, nh): e.tensor_tensor(
                     out=a, in0=p_, in1=m_, op=OP.mult))
            yield
            tr3(tA, lambda i: col(fpow[0], i), wait=[('t_v', P.v('t_v'))],
                tag=f'fT-{h0}-{j}')
            ev_s(tA, lambda e, S, a=r3(ftc[0], nh), p_=tA.view(nh, 128):
                 e.copy(a, p_))
            mcur = tb3['mb'][0]
            tA.r = P.vec(lambda e, S, a=r3(mcur, nh), p_=tA.view(nh, 128),
                         i_=bcm(ident, nh): e.tensor_tensor(
                             out=a, in0=i_, in1=p_, op=OP.add),
                         wait=[('t_s', P.v('t_s'))])
            yield
            cur_ft = ftc[0]
            for l in range(1, LMAX + 1):
                pw = ('t_v', P.v('t_v'))
                sw = ('t_s', P.v('t_s'))
                mm3(tA, lambda i, _f=cur_ft: col(_f, i),
                    lambda i, _g=fpow[l - 1]: col(_g, i),
                    wait=[pw, sw], tag=f'sq-{h0}-{j}-{l}')
                ev_s(tA, lambda e, S, a=r3(fpow[l], nh), p_=tA.view(nh, 128):
                     e.copy(a, p_))
                mnxt = tb3['mb'][l % 2]
                mm3(tB, lambda i, _g=fpow[l]: col(_g, i),
                    lambda i, _m=mcur: col(_m, i),
                    wait=[('t_s', P.v('t_s')), pw], tag=f'M-{h0}-{j}-{l}')
                ev_v(tB, lambda e, S, a=r3(mnxt, nh), m_=r3(mcur, nh),
                     p_=tB.view(nh, 128): e.tensor_tensor(
                         out=a, in0=m_, in1=p_, op=OP.add))
                mcur = mnxt
                yield
                if l < LMAX:
                    nft = ftc[l % 2]
                    tr3(tA, lambda i, _g=fpow[l]: col(_g, i),
                        wait=[('t_s', P.v('t_s'))], tag=f'fT-{h0}-{j}-{l}')
                    ev_s(tA, lambda e, S, a=r3(nft, nh), p_=tA.view(nh, 128):
                         e.copy(a, p_))
                    cur_ft = nft
                    yield
            tm = tb3['k_rt']
            tr3(tA, lambda i, _m=mcur: col(_m, i), wait=[('t_v', P.v('t_v'))],
                tag=f'mT-{h0}-{j}')
            ev_s(tA, lambda e, S, a=r3(tm, nh), p_=tA.view(nh, 128): e.copy(a, p_))
            yield
            mm3(tB, lambda i: col(tm, i),
                lambda i: tb3['u_a'][:, i * 256:(i + 1) * 256],
                wait=[('t_s', P.v('t_s')), ('t_v', P.v('t_v'))], width=256,
                tag=f'ap-{h0}-{j}')
            ev_v(tB, lambda e, S, a=r3(tb3['bTV'], nh), p_=tB.view(nh, 128),
                 b_=bc1(bslc, nh): e.tensor_tensor(
                     out=a, in0=p_, in1=b_, op=OP.mult))
            tB.r = P.vec(lambda e, S, a=r3(tb3['y3'], nh),
                         p_=tB.view(nh, 128, off=128), b_=bc1(bslc, nh):
                         e.tensor_tensor(out=a, in0=p_, in1=b_, op=OP.mult))
            yield
            yt = tb3['qn']
            tr3(tA, lambda i: col(tb3['y3'], i), wait=[('t_v', P.v('t_v'))],
                tag=f'yT-{h0}-{j}')
            ev_s(tA, lambda e, S, a=r3(yt, nh), p_=tA.view(nh, 128): e.copy(a, p_))
            mm3(tB, lambda i: col(tb3['y3'], i),
                lambda i: tb3['u_a'][:, i * 256 + 128:(i + 1) * 256],
                wait=[('t_v', P.v('t_v'))], tag=f'C1-{h0}-{j}')
            ev_v(tB, lambda e, S, a=r3(tb3['m1'], nh), p_=tB.view(nh, 128),
                 i_=bcm(ident, nh): e.tensor_tensor(
                     out=a, in0=i_, in1=p_, op=OP.subtract))
            yield
            if j == 0:
                uhat = tb3['bTV']
            else:
                uhat = tb3['q_rt']
                mm3(tB, lambda i: col(yt, i), lambda i: col(Dprev, i),
                    wait=[('t_s', P.v('t_s')), ('t_v', P.v('t_v'))],
                    tag=f'YD-{h0}-{j}')
                ev_v(tB, lambda e, S, a=r3(uhat, nh), b_=r3(tb3['bTV'], nh),
                     p_=tB.view(nh, 128): e.tensor_tensor(
                         out=a, in0=b_, in1=p_, op=OP.subtract))
            mm3(tB, lambda i: col(tb3['y3'], i), lambda i: col(tb3['gm'], i),
                wait=[('t_v', P.v('t_v')), ('t_s', P.v('t_s'))],
                tag=f'YG-{h0}-{j}')
            ev_v(tB, lambda e, S, a=r3(tb3['qpre'], nh), q_=r3(tb3['q_fm'], nh),
                 p_=tB.view(nh, 128): e.tensor_tensor(
                     out=a, in0=q_, in1=p_, op=OP.subtract),
                 wait=[pd_guard] if j <= 1 else [])
            yield
            gv = [('t_v', P.v('t_v')), ('t_s', P.v('t_s'))]
            if j == 0:
                mm3(tB, lambda i: col(tb3['gm'], i), lambda i: col(uhat, i),
                    wait=gv, tag=f'O-{h0}-{j}')
            else:
                mm3(tB, lambda i: col(tb3['q_fm'], i), lambda i: col(Dprev, i),
                    wait=gv, tag=f'O-{h0}-{j}',
                    pair_rhs=(lambda i: col(tb3['gm'], i), lambda i: col(uhat, i)))
            ev_s(tB, lambda e, S, a=r3(tb3['ost'], nh), p_=tB.view(nh, 128):
                 e.copy(a, p_), wait=[('d_oq', P.v('d_oq'))])
            mm3(tB, lambda i: tb3['u_a'][:, i * 256 + 128:(i + 1) * 256],
                lambda i: col(uhat, i), wait=gv, tag=f'Du-{h0}-{j}')
            g_ = [pd_guard] if j <= 1 else []
            if j == 0:
                ev_s(tB, lambda e, S, a=r3(Dnew, nh), p_=tB.view(nh, 128):
                     e.copy(a, p_), wait=g_)
            else:
                ev_v(tB, lambda e, S, a=r3(Dnew, nh), d_=r3(Dprev, nh),
                     p_=tB.view(nh, 128): e.tensor_tensor(
                         out=a, in0=d_, in1=p_, op=OP.add), wait=g_)
            yield
            if j == 0:
                P.sca(lambda e, S, a=tb3['qst'][:, 0:nh * 128],
                      b=tb3['qpre'][:, 0:nh * 128]: e.copy(a, b),
                      wait=[('d_oq', P.v('d_oq')), ('t_v', P.v('t_v'))])
            else:
                mm3(tA, lambda i: col(Pprev, i), lambda i: col(tb3['qpre'], i),
                    wait=[('t_v', P.v('t_v'))], tag=f'Qh-{h0}-{j}')
                ev_s(tA, lambda e, S, a=r3(tb3['qst'], nh), p_=tA.view(nh, 128):
                     e.copy(a, p_), wait=[('d_oq', P.v('d_oq'))])
            if j == 0:
                mm3(tA, lambda i: col(tb3['m1'], i), lambda i: ident,
                    wait=[('t_v', P.v('t_v'))], tag=f'Pu-{h0}-{j}')
            else:
                mm3(tA, lambda i: col(tb3['m1'], i), lambda i: col(Pprev, i),
                    wait=[('t_v', P.v('t_v'))], tag=f'Pu-{h0}-{j}')
            ev_s(tA, lambda e, S, a=r3(Pnew, nh), p_=tA.view(nh, 128):
                 e.copy(a, p_), wait=g_)
            for i in range(nh):
                h = h0 + i
                ow = dram['odeg_scr'][(h * NCH + j) * 128:(h * NCH + j + 1) * 128, :]
                qw = dram['qhat_scr'][(h * NCH + j) * 128:(h * NCH + j + 1) * 128, :]
                P.dma('gpsimd', ow, col(tb3['ost'], i),
                      wait=[('t_s', P.v('t_s'))] if i == 0 else (), sem='d_oq')
                P.dma('gpsimd', qw, col(tb3['qst'], i), sem='d_oq')
            yield
        Pfin = tb3['P'][(NCH - 1) % 2]; Dfin = tb3['D'][(NCH - 1) % 2]
        tr3(tA, lambda i: col(Pfin, i),
            wait=[('t_v', P.v('t_v')), ('t_s', P.v('t_s')),
                  ('d_pd', P.v('d_pd'))], tag=f'pT-{h0}')
        ev_v(tA, lambda e, S, a=r3(tb3['qpre'], nh), p_=tA.view(nh, 128):
             e.tensor_copy(out=a, in_=p_))
        for i in range(nh):
            h = h0 + i
            P.dma('gpsimd', dram['pd_loc'][(h * 2) * 128:(h * 2 + 1) * 128, :],
                  col(tb3['qpre'], i),
                  wait=[('t_v', P.v('t_v'))] if i == 0 else (), sem='d_pd')
            P.dma('gpsimd', dram['pd_loc'][(h * 2 + 1) * 128:(h * 2 + 2) * 128, :],
                  col(Dfin, i), sem='d_pd')

    def chain(*gs):
        for g_ in gs:
            yield from g_

    def run_rr(gens):
        live = list(gens)
        while live:
            nxt = []
            for g in live:
                try:
                    next(g)
                    nxt.append(g)
                except StopIteration:
                    pass
            live = nxt

    run_rr([chain(*[proj_gen(h2) for h2 in range(0, min(NI, NH))])])
    for g0 in range(0, NH, NI):
        nh = min(NI, NH - g0)
        gens = [triple_gen(g0, nh)]
        nxt_heads = list(range(g0 + NI, min(g0 + 2 * NI, NH)))
        if nxt_heads:
            gens.append(chain(*[proj_gen(h2) for h2 in nxt_heads]))
        run_rr(gens)

    for co in range(NK):
        dw = P.dma('sync', wnat, dram['wo'][co * 128:(co + 1) * 128, :],
                   wait=[('t_t', P.v('t_t')), ('t_s', P.v('t_s'))])
        for k in range(NK):
            bkw = banks[1 + (k % 2)]
            pv = transpose_to(bkw, wnat[:, k * 128:(k + 1) * 128],
                              wait=[dw, ('d_scalar', P.v('d_scalar'))])
            evacs(bkw, lambda e, S, p_=pv: e.activation(
                out=woh, in_=p_, func=AF.Identity, scale=T['rmsw']),
                wait=[('d_scalar', P.v('d_scalar'))])
            evacv(bkw, lambda e, S, p_=pv: e.scalar_tensor_tensor(
                out=wol, in0=p_, scalar=T['rmsw'], in1=woh,
                op0=OP.mult, op1=OP.subtract),
                wait=[('t_s', P.v('t_s')), ('d_scalar', P.v('d_scalar'))])
            r = (co * NK + k) * 128
            P.dma('scalar', dram['wo_hi_scr'][r:r + 128, :], woh,
                  wait=[('t_s', P.v('t_s'))])
            P.dma('scalar', dram['wo_lo_scr'][r:r + 128, :], wol,
                  wait=[('t_v', P.v('t_v'))])
    A.pop(); AB.pop()




def emit_p34(P, c, dram, T):
    NK, DM, FH, SPAN, PSPAN = c['NK'], c['DM'], c['FH'], c['SPAN'], c['PSPAN']
    NH, NCH, NCORES = c['NH'], c['NCH'], c['NCORES']
    HSUP = min(4, c['FHC'])
    NSUP = c['FHC'] // HSUP
    A, AB, banks = T['A'], T['AB'], T['banks']
    xn_hi = T['xn_hi']
    ident = T['ident']
    eps = float(np.float32(np.finfo(np.float32).eps))
    for s_ in ('d_xl', 'd_xlo3', 'd_w1a', 'd_w1b', 'd_w2a', 'd_w2b', 'd_g',
               'd_pda', 'd_oq2', 'd_oc', 'd_os0', 'd_os1', 'd_wo', 'd_xg', 'd_out'):
        P.sem(s_)

    def tmm(bank, lhsT, rhs, start, stop, wait=(), width=512, rows=128, tag=''):
        ws = [x for x in wait if x]
        if start:
            ws += bank.wwait()
        return P.ten(lambda e, S, a_=bank.ap[0:rows, 0:width], l_=lhsT, r_=rhs,
                     st=start, sp=stop: e.matmul(a_, l_, r_, start=st, stop=sp),
                     wait=ws, tag=tag)

    def mark_w(bank):
        bank.w = ('t_t', P.v('t_t'))

    def evacv(bank, fn, wait=()):
        bank.r = P.vec(fn, wait=list(bank.rwait()) + list(wait))
        return bank.r

    def evacs(bank, fn, wait=()):
        bank.r = P.sca(fn, wait=list(bank.rwait()) + list(wait))
        return bank.r

    def transpose_to(bank, src, n=128, wait=()):
        ws = bank.wwait() + [x for x in wait if x]
        pv = bank.ap[:, 0:n]
        bank.w = P.ten(lambda e, S, p_=pv, s_=src, i_=ident[0:n, 0:n]:
                       e.transpose(p_, s_, i_), wait=ws)
        return pv

    A.push(); AB.push()
    w1b2 = (AB.alloc(NK * HSUP * 128, 'w1a'), AB.alloc(NK * HSUP * 128, 'w1b'))
    xlo3 = AB.alloc(NK * 512, 'xlo3')
    w2b2 = (AB.alloc(HSUP * DM // 2, 'w2a'), AB.alloc(HSUP * DM // 2, 'w2b'))
    yhi = AB.alloc(HSUP * 512, 'yhi')
    ylo = AB.alloc(HSUP * 512, 'ylo')
    rbuf2 = (A.alloc(512, 'rbuf_a'), A.alloc(512, 'rbuf_b'))
    ybuf = A.alloc(512, 'ybuf')
    zbuf = A.alloc(512, 'zbuf')
    out_acc = A.alloc(NK * 512, 'out_acc')
    sr = A.alloc(512, 'sr')
    rtok = A.alloc(8, 'rtok')
    gbuf = A.alloc(DM, 'gbuf')
    HHALF = min(4, NH)
    pdb = A.alloc(HHALF * 2 * 128, 'pdb')
    sig_a = A.alloc(NH * 128, 'sig_a')
    stmp = A.alloc(128, 'stmp')
    ob = A.alloc(256, 'ob')
    qb = A.alloc(256, 'qb')
    orw = A.alloc(128, 'orw')
    sqt = A.alloc(128, 'sqt')
    nst = A.alloc(8, 'nst')
    onr = A.alloc(128, 'onr')
    ohi_st = AB.alloc(128, 'ohi_st')
    olo_st = AB.alloc(128, 'olo_st')

    def ffn_gen():
        bh2 = (banks[0], banks[1])
        bs, bo, bt = banks[7], banks[2], banks[3]
        cc_wait = ('cc', P.v('cc'))
        entry_v = ('t_v', P.v('t_v'))
        entry_s = ('t_s', P.v('t_s'))
        for (tc0, tw) in c['TOKSUBS']:
            dxl = P.dma('gpsimd',
                        xlo3.rearrange("p (k f) -> p k f", f=512)[:, :, 0:tw],
                        dram['xnl_scr'].rearrange("(k p) col -> p k col", p=128)
                        [:, :, 3 + tc0:3 + tc0 + tw],
                        wait=[('t_t', P.v('t_t')), ('d_xl', P.v('d_xl')),
                              entry_v, entry_s],
                        sem='d_xlo3')
            ss_open = False
            HW = HSUP * 128
            DH = DM // 2
            wdma = {}
            wdma2 = {}

            def wpf(hs_):
                if hs_ < NSUP and hs_ not in wdma:
                    wdma[hs_] = P.dma('sync',
                              w1b2[hs_ % 2].rearrange("p (k f) -> p k f", f=HW),
                              dram['w1t'].rearrange("(k p) f -> p k f", p=128)
                              [:, :, hs_ * HW:(hs_ + 1) * HW],
                              wait=[cc_wait, entry_v, entry_s,
                                    ('t_t', P.v('t_t'))],
                              sem='d_w1a' if hs_ % 2 == 0 else 'd_w1b')

            def wpf2(hs_, half):
                if hs_ < NSUP and (hs_, half) not in wdma2:
                    wdma2[(hs_, half)] = P.dma('scalar',
                              w2b2[half].rearrange("p (i f) -> p i f", f=DH),
                              dram['w2t'][hs_ * HSUP * 128:(hs_ + 1) * HSUP * 128,
                                          half * DH:(half + 1) * DH]
                              .rearrange("(i p) f -> p i f", p=128),
                              wait=[cc_wait, entry_v, entry_s,
                                    ('t_t', P.v('t_t'))],
                              sem='d_w2a' if half == 0 else 'd_w2b')
            wpf(0)
            wpf2(0, 0)
            for hs in range(NSUP):
                wall = wdma[hs]
                wpf(hs + 1)
                wpf2(hs, 1)
                w1buf = w1b2[hs % 2]
                for i in range(HSUP):
                    bh = bh2[i % 2]
                    rbuf = rbuf2[i % 2]
                    nmm = NK * 2
                    im = 0
                    for k in range(NK):
                        wap = w1buf[:, k * HSUP * 128 + i * 128:
                                    k * HSUP * 128 + (i + 1) * 128]
                        for hx in (1, 0):
                            xap = (xn_hi[:, k * PSPAN + 3 + tc0:k * PSPAN + 3 + tc0 + tw]
                                   if hx else xlo3[:, k * 512:k * 512 + tw])
                            tmm(bh, wap, xap, start=(im == 0), stop=(im == nmm - 1),
                                wait=[wall, dxl] if im == 0 else (), width=tw,
                                tag=f'fc-t{tc0}-{hs}-{i}')
                            im += 1
                    mark_w(bh)
                    bh.r = P.sca(lambda e, S, a=rbuf[:, 0:tw], p_=bh.ap[:, 0:tw]:
                                 e.activation(out=a, in_=p_, func=AF.Relu),
                                 wait=bh.rwait())
                    P.vec(lambda e, S, a=ybuf[:, 0:tw], b=rbuf[:, 0:tw]:
                          e.tensor_tensor(out=a, in0=b, in1=b, op=OP.mult),
                          wait=[('t_s', P.v('t_s'))])
                    P.sca(lambda e, S, a=zbuf[:, 0:tw], b=ybuf[:, 0:tw]:
                          e.activation(out=a, in_=b, func=AF.Square),
                          wait=[('t_v', P.v('t_v'))])
                    P.vec(lambda e, S, a=yhi[:, i * 512:i * 512 + tw], b=ybuf[:, 0:tw]:
                          e.tensor_copy(out=a, in_=b))
                    P.vec(lambda e, S, a=ylo[:, i * 512:i * 512 + tw], b=ybuf[:, 0:tw],
                          h_=yhi[:, i * 512:i * 512 + tw]: e.tensor_tensor(
                              out=a, in0=b, in1=h_, op=OP.subtract))
                    tmm(bs, T['onesc'], zbuf[:, 0:tw], start=(not ss_open),
                        stop=(hs == NSUP - 1 and i == HSUP - 1),
                        wait=[('t_s', P.v('t_s'))], width=tw, rows=1,
                        tag=f'ss-t{tc0}-{hs}-{i}')
                    ss_open = True
                    mark_w(bs)
                    yield
                ydone = ('t_v', P.v('t_v'))
                for dk in range(NK):
                    half = (dk * 128) // DH
                    w2buf = w2b2[half]
                    whalf = wdma2[(hs, half)]
                    if dk * 128 == DH:
                        wpf2(hs + 1, 0)
                    nmm = HSUP * 2
                    im = 0
                    for i in range(HSUP):
                        dko = dk * 128 - half * DH
                        wap = w2buf[:, i * DH + dko:i * DH + dko + 128]
                        for yy in (yhi, ylo):
                            tmm(bo, wap, yy[:, i * 512:i * 512 + tw],
                                start=(im == 0), stop=(im == nmm - 1),
                                wait=[ydone, whalf] if im == 0 else (), width=tw,
                                tag=f'pr-t{tc0}-{hs}-{dk}')
                            im += 1
                    mark_w(bo)
                    if hs == 0:
                        bo.r = P.vec(lambda e, S, a=out_acc[:, dk * 512:dk * 512 + tw],
                                     p_=bo.ap[:, 0:tw]: e.tensor_copy(out=a, in_=p_),
                                     wait=list(bo.rwait()) + [('t_t', P.v('t_t'))])
                    else:
                        bo.r = P.vec(lambda e, S, a=out_acc[:, dk * 512:dk * 512 + tw],
                                     p_=bo.ap[:, 0:tw]: e.tensor_tensor(
                                         out=a, in0=a, in1=p_, op=OP.add),
                                     wait=bo.rwait())
                    if dk % 4 == 3:
                        yield
            bs.r = P.vec(lambda e, S, a=sr[0:1, 0:tw], p_=bs.ap[0:1, 0:tw]:
                         e.tensor_scalar(out=a, in0=p_,
                                         scalar1=float(np.float32(1.0 / FH)),
                                         scalar2=eps, op0=OP.mult, op1=OP.add),
                         wait=bs.rwait())
            P.sca(lambda e, S, a=sr[0:1, 0:tw]: e.activation(
                out=a, in_=a, func=AF.Sqrt), wait=[('t_v', P.v('t_v'))])
            P.vec(lambda e, S, a=sr[0:1, 0:tw]: e.reciprocal(
                out=a, in_=a), wait=[('t_s', P.v('t_s'))])
            rdone = ('t_v', P.v('t_v'))
            for tg in range(tw // 128):
                pv = bt.ap[:, 0:1]
                bt.w = P.ten(lambda e, S, p_=pv, s_=sr[0:1, tg * 128:(tg + 1) * 128],
                             i_=T['ident'][0:1, 0:1]: e.transpose(p_, s_, i_),
                             wait=bt.wwait() + [rdone], tag=f'rT-t{tc0}-{tg}')
                bt.r = P.vec(lambda e, S, a=rtok[:, 0:1], p_=pv:
                             e.tensor_copy(out=a, in_=p_),
                             wait=list(bt.rwait()) + [('t_s', P.v('t_s')),
                                                      ('d_g', P.v('d_g'))])
                for dk in range(NK):
                    pv2 = bt.ap[:, 4:132]
                    bt.w = P.ten(lambda e, S, p_=pv2,
                                 s_=out_acc[:, dk * 512 + tg * 128:dk * 512 + (tg + 1) * 128],
                                 i_=T['ident']: e.transpose(p_, s_, i_),
                                 wait=bt.wwait() + [('t_v', P.v('t_v'))],
                                 tag=f'oT-t{tc0}-{tg}-{dk}')
                    bt.r = P.vec(lambda e, S, a=gbuf[:, dk * 128:(dk + 1) * 128],
                                 p_=pv2: e.tensor_scalar(
                                     out=a, in0=p_, scalar1=rtok[:, 0:1],
                                     scalar2=None, op0=OP.mult),
                                 wait=bt.rwait())
                P.sca(lambda e, S, a=gbuf: e.activation(out=a, in_=a, func=AF.Silu),
                      wait=[('t_v', P.v('t_v')), ('d_g', P.v('d_g'))])
                P.dma('gpsimd', dram['g_scr'][tc0 + tg * 128:tc0 + (tg + 1) * 128, :],
                      gbuf, wait=[('t_s', P.v('t_s'))], sem='d_g')
                yield

    def p4ab_gen():
        b0, b1, b2 = banks[5], banks[4], banks[6]
        P.vec(lambda e, S: e.memset(sig_a, 0.0))
        for c2 in range(NCORES):
            m_ap = T['sigmask'][:, c2:c2 + 1]
            for hh in range(NH // HHALF):
                r0 = (c2 * NH + hh * HHALF) * 2 * 128
                dwp = P.dma('gpsimd',
                            pdb.rearrange("p (b d) -> p b d", d=128),
                            dram['pd_all'][r0:r0 + HHALF * 2 * 128, :]
                            .rearrange("(b p) d -> p b d", p=128),
                            wait=[('cc', P.v('cc')), ('t_v', P.v('t_v'))],
                            sem='d_pda')
                for hl in range(HHALF):
                    h = hh * HHALF + hl
                    pt = pdb[:, (hl * 2) * 128:(hl * 2 + 1) * 128]
                    dd = pdb[:, (hl * 2 + 1) * 128:(hl * 2 + 2) * 128]
                    ss = sig_a[:, h * 128:(h + 1) * 128]
                    tmm(b0, pt, ss, start=True, stop=True,
                        wait=[dwp, ('t_v', P.v('t_v'))], width=128,
                        tag=f'PS-{c2}-{h}')
                    mark_w(b0)
                    b0.r = P.vec(lambda e, S, a=orw, d_=dd, p_=b0.ap[:, 0:128]:
                                 e.tensor_tensor(out=a, in0=d_, in1=p_, op=OP.add),
                                 wait=b0.rwait())
                    P.vec(lambda e, S, a=orw, s_=ss: e.tensor_tensor(
                        out=a, in0=a, in1=s_, op=OP.subtract))
                    P.vec(lambda e, S, a=stmp, t_=orw, m_=m_ap, s_=ss:
                          e.scalar_tensor_tensor(out=a, in0=t_, scalar=m_, in1=s_,
                                                 op0=OP.mult, op1=OP.add))
                    P.vec(lambda e, S, a=ss: e.tensor_copy(out=a, in_=stmp))
                    yield
        sig_done = ('t_v', P.v('t_v'))
        for sp in range(SPAN // 256):
            for h in range(NH):
                r0 = (h * NCH + sp * 2) * 128
                P.dma('gpsimd', ob.rearrange("p (b d) -> p b d", d=128),
                      dram['odeg_scr'][r0:r0 + 256, :]
                      .rearrange("(b p) d -> p b d", p=128),
                      wait=[('t_v', P.v('t_v')), ('t_s', P.v('t_s')),
                            ('t_t', P.v('t_t'))], sem='d_oq2')
                P.dma('gpsimd', qb.rearrange("p (b d) -> p b d", d=128),
                      dram['qhat_scr'][r0:r0 + 256, :]
                      .rearrange("(b p) d -> p b d", p=128), sem='d_oq2')
                oq2w = ('d_oq2', P.v('d_oq2'))
                for jj in range(2):
                    tmm(b1, qb[:, jj * 128:(jj + 1) * 128],
                        sig_a[:, h * 128:(h + 1) * 128], start=True, stop=True,
                        wait=[oq2w, sig_done], width=128, tag=f'corr-{sp}-{h}-{jj}')
                    mark_w(b1)
                    b1.r = P.vec(lambda e, S, a=orw, o_=ob[:, jj * 128:(jj + 1) * 128],
                                 p_=b1.ap[:, 0:128]: e.tensor_tensor(
                                     out=a, in0=o_, in1=p_, op=OP.add),
                                 wait=list(b1.rwait()) + [oq2w])
                    P.vec(lambda e, S: e.tensor_tensor(
                        out=sqt, in0=orw, in1=orw, op=OP.mult))
                    P.vec(lambda e, S: e.tensor_reduce(
                        out=nst[:, 0:1], in_=sqt, op=OP.add, axis=AX.X))
                    P.vec(lambda e, S: e.tensor_scalar(
                        out=nst[:, 1:2], in0=nst[:, 0:1],
                        scalar1=float(np.float32(1.0 / 128)), scalar2=1e-5,
                        op0=OP.mult, op1=OP.add))
                    P.sca(lambda e, S: e.activation(
                        out=nst[:, 2:3], in_=nst[:, 1:2], func=AF.Sqrt),
                        wait=[('t_v', P.v('t_v'))])
                    P.vec(lambda e, S: e.reciprocal(out=nst[:, 3:4], in_=nst[:, 2:3]),
                          wait=[('t_s', P.v('t_s'))])
                    P.vec(lambda e, S: e.tensor_scalar(
                        out=onr, in0=orw, scalar1=nst[:, 3:4], scalar2=None,
                        op0=OP.mult))
                    pv = b2.ap[:, 0:128]
                    b2.w = P.ten(lambda e, S, p_=pv: e.transpose(p_, onr, ident),
                                 wait=b2.wwait() + [('t_v', P.v('t_v'))],
                                 tag=f'ocT-{sp}-{h}-{jj}')
                    b2.r = P.sca(lambda e, S, a=ohi_st, p_=pv: e.copy(a, p_),
                                 wait=list(b2.rwait()) + [('d_oc', P.v('d_oc'))])
                    b2.r = P.vec(lambda e, S, a=olo_st, p_=pv, h_=ohi_st:
                                 e.tensor_tensor(out=a, in0=p_, in1=h_,
                                                 op=OP.subtract),
                                 wait=list(b2.rwait()) + [('t_s', P.v('t_s')),
                                                          ('d_oc', P.v('d_oc'))])
                    cbase = h * SPAN + sp * 256 + jj * 128
                    P.dma('gpsimd', dram['oc_hi'][:, cbase:cbase + 128], ohi_st,
                          wait=[('t_s', P.v('t_s')), ('t_v', P.v('t_v'))],
                          sem='d_oc')
                    P.dma('gpsimd', dram['oc_lo'][:, cbase:cbase + 128], olo_st,
                          wait=[('t_v', P.v('t_v'))], sem='d_oc')
                    yield

    import os as _os
    if _os.environ.get('P34_SEQ'):
        for g in (ffn_gen(), p4ab_gen()):
            for _ in g:
                pass
    else:
        gens = [ffn_gen(), p4ab_gen()]
        live = list(gens)
        while live:
            nxt = []
            for g in live:
                try:
                    next(g)
                    nxt.append(g)
                except StopIteration:
                    pass
            live = nxt
    A.pop(); AB.pop()
    oc_done = ('d_oc', P.v('d_oc'))
    g_done = ('d_g', P.v('d_g'))

    A.push(); AB.push()
    ohi_sl2 = (AB.alloc(NK * 512, 'ohi_sla'), AB.alloc(NK * 512, 'ohi_slb'))
    olo_sl2 = (AB.alloc(NK * 512, 'olo_sla'), AB.alloc(NK * 512, 'olo_slb'))
    KH = max(1, NK // 2)
    woh2 = [AB.alloc(KH * 128, f'woh{x}') for x in range(NK // KH)]
    wol2 = [AB.alloc(KH * 128, f'wol{x}') for x in range(NK // KH)]
    sfmb = A.alloc(512, 'sfmb')
    s_tok = [A.alloc(DM, f's_tok{tg}') for tg in range(4)]
    gt = A.alloc(512, 'gt')
    xt = A.alloc(512, 'xt')
    obuf = A.alloc(512, 'obuf')
    bw, bts = banks[0], banks[3]
    osl = {}

    def osl_pf(s2):
        if s2 < SPAN // 512 and s2 not in osl:
            sm = 'd_os%d' % (s2 % 2)
            P.dma('sync', ohi_sl2[s2 % 2].rearrange("p (k f) -> p k f", f=512),
                  dram['oc_hi'].rearrange("p (k f) -> p k f", f=SPAN)
                  [:, :, s2 * 512:(s2 + 1) * 512],
                  wait=[oc_done, ('t_t', P.v('t_t'))], sem=sm)
            P.dma('scalar', olo_sl2[s2 % 2].rearrange("p (k f) -> p k f", f=512),
                  dram['oc_lo'].rearrange("p (k f) -> p k f", f=SPAN)
                  [:, :, s2 * 512:(s2 + 1) * 512],
                  wait=[oc_done, ('t_t', P.v('t_t'))], sem=sm)
            osl[s2] = (sm, P.v(sm))
    osl_pf(0)
    for sp2 in range(SPAN // 512):
        ohi_sl = ohi_sl2[sp2 % 2]
        olo_sl = olo_sl2[sp2 % 2]
        osw = osl.pop(sp2)
        osl_pf(sp2 + 1)
        for co in range(NK):
            nmm = NK * 3
            im = 0
            wows = []
            for kh in range(NK // KH):
                r0 = (co * NK + kh * KH) * 128
                P.dma('sync', woh2[kh].rearrange("p (b d) -> p b d", d=128),
                      dram['wo_hi_scr'][r0:r0 + KH * 128, :]
                      .rearrange("(b p) d -> p b d", p=128),
                      wait=[('t_t', P.v('t_t'))], sem='d_wo')
                P.dma('scalar', wol2[kh].rearrange("p (b d) -> p b d", d=128),
                      dram['wo_lo_scr'][r0:r0 + KH * 128, :]
                      .rearrange("(b p) d -> p b d", p=128),
                      wait=[('t_t', P.v('t_t'))], sem='d_wo')
                wows.append(('d_wo', P.v('d_wo')))
            for kh in range(NK // KH):
                woh = woh2[kh]
                wol = wol2[kh]
                wow = wows[kh]
                for kl in range(KH):
                    k = kh * KH + kl
                    for wch, och in ((woh, ohi_sl), (woh, olo_sl), (wol, ohi_sl)):
                        tmm(bw, wch[:, kl * 128:(kl + 1) * 128],
                            och[:, k * 512:(k + 1) * 512],
                            start=(im == 0), stop=(im == nmm - 1),
                            wait=([osw, wow]) if im == 0 else [wow] if kl == 0 else (),
                            width=512, tag=f'wo-{sp2}-{co}')
                        im += 1
            mark_w(bw)
            for tg in range(4):
                bw.r = P.vec(lambda e, S, a=sfmb[:, tg * 128:(tg + 1) * 128],
                             p_=bw.ap[:, tg * 128:(tg + 1) * 128]:
                             e.tensor_copy(out=a, in_=p_),
                             wait=list(bw.rwait()) + ([('t_t', P.v('t_t'))]
                                                      if tg == 0 else []))
                pv = bts.ap[:, 0:128]
                bts.w = P.ten(lambda e, S, p_=pv,
                              s_=sfmb[:, tg * 128:(tg + 1) * 128]:
                              e.transpose(p_, s_, ident),
                              wait=bts.wwait() + [('t_v', P.v('t_v'))],
                              tag=f'sT-{sp2}-{co}-{tg}')
                bts.r = P.sca(lambda e, S, a=s_tok[tg][:, co * 128:(co + 1) * 128],
                              p_=pv: e.copy(a, p_),
                              wait=list(bts.rwait()) +
                              ([('t_v', P.v('t_v'))] if co == 0 else []))
        sdone = [('t_s', P.v('t_s')), ('t_v', P.v('t_v'))]
        for tg in range(4):
            trow = sp2 * 512 + tg * 128
            for p0_ in range(0, DM, 512):
                dg = P.dma('scalar', gt, dram['g_scr'][trow:trow + 128, p0_:p0_ + 512],
                           wait=[g_done, ('d_xg', P.v('d_xg')),
                                 ('t_v', P.v('t_v'))] + sdone, sem='d_xg')
                P.dma('scalar', xt, dram['x_span'][trow:trow + 128, p0_:p0_ + 512],
                      sem='d_xg')
                xgw = ('d_xg', P.v('d_xg'))
                P.vec(lambda e, S, s_=s_tok[tg][:, p0_:p0_ + 512]: e.tensor_tensor(
                    out=obuf, in0=s_, in1=gt, op=OP.mult),
                    wait=[xgw, ('d_out', P.v('d_out'))])
                P.vec(lambda e, S: e.tensor_tensor(
                    out=obuf, in0=obuf, in1=xt, op=OP.add))
                P.dma('gpsimd', dram['out_span'][trow:trow + 128, p0_:p0_ + 512],
                      obuf, wait=[('t_v', P.v('t_v'))], sem='d_out')
    P.emit('gpsimd', None, wait=[('d_out', P.v('d_out'))])
    A.pop(); AB.pop()

HD_SC = float(np.float32(128.0 ** -0.5))
LMAX = 6

CFG = dict(B=2, S=8192, NH=16, HD=128, FH=8192, NCORES=8)


def kernel(**inputs):
    from concourse.bass_utils import run_bass_kernel_spmd
    c, in_maps = host_prepare(CFG, inputs)
    nc, _ = build_nc(CFG)
    r = run_bass_kernel_spmd(nc, in_maps, core_ids=list(range(c['NCORES'])))
    out = np.concatenate([r.results[i]['out_span'] for i in range(c['NCORES'])],
                         axis=0)
    return out.reshape(c['B'], c['S'], c['DM'])
